# version 19
# baseline (speedup 1.0000x reference)
import numpy as np
import concourse.bass as bass
import concourse.mybir as mybir
from concourse.bass_utils import run_bass_kernel_spmd

F32 = mybir.dt.float32
BF16 = mybir.dt.bfloat16
I32 = mybir.dt.int32
Alu = mybir.AluOpType
Act = mybir.ActivationFunctionType
AX = mybir.AxisListType

C = 1024
TQ = 512
SUB = 128
NSUB = TQ // SUB
NCH = C // 128
INV_LN2 = float(1.0 / np.log(2.0))
LN2_HI = float(np.float32(np.log(2.0)))
LN2_LO = float(np.float64(np.log(2.0)) - np.float64(np.float32(np.log(2.0))))
EPS = 1e-5
TINY = 1e-30
THRS = [(0.25, Alu.is_gt, 0.5), (0.75, Alu.is_ge, 0.5), (1.25, Alu.is_gt, 0.5),
        (1.75, Alu.is_ge, 0.5), (2.5, Alu.is_gt, 1.0), (3.5, Alu.is_ge, 1.0),
        (5.0, Alu.is_gt, 2.0)]


def build_nc(T):
    NT = T // TQ
    nc = bass.Bass("TRN2", target_bir_lowering=False)

    x_d = nc.dram_tensor("x", [T, C], F32, kind="ExternalInput").ap()
    wqkv_d = nc.dram_tensor("w_qkv", [C, 3 * C], F32, kind="ExternalInput").ap()
    bqkv_d = nc.dram_tensor("b_qkv", [3 * C], F32, kind="ExternalInput").ap()
    lnw_d = nc.dram_tensor("ln_w", [C], F32, kind="ExternalInput").ap()
    lnb_d = nc.dram_tensor("ln_b", [C], F32, kind="ExternalInput").ap()
    fw_d = nc.dram_tensor("f_w", [C, C], F32, kind="ExternalInput").ap()
    fb_d = nc.dram_tensor("f_b", [C], F32, kind="ExternalInput").ap()
    dw_d = nc.dram_tensor("d_w", [C, 32], F32, kind="ExternalInput").ap()
    da_d = nc.dram_tensor("d_a", [1], F32, kind="ExternalInput").ap()
    out_d = nc.dram_tensor("out", [T, C], F32, kind="ExternalOutput").ap()
    wqk_d = nc.dram_tensor("wqk", [2, 16, 128, NCH, 128], BF16, kind="Internal").ap()
    wv_d = nc.dram_tensor("wvs", [2, 2, 128, NCH, 512], BF16, kind="Internal").ap()

    import contextlib
    with contextlib.ExitStack() as es:
        def sbt(name, shape, dt):
            return es.enter_context(nc.sbuf_tensor(name, shape, dt))

        wqT_h = sbt("wqT_h", [128, 8 * C], BF16)
        wqT_l = sbt("wqT_l", [128, 8 * C], BF16)
        kh = sbt("kh", [128, NCH * 768], BF16)
        kl = sbt("kl", [128, NCH * 768], BF16)
        vh = sbt("vh", [128, 6 * C], BF16)
        vl = sbt("vl", [128, 6 * C], BF16)
        wmh = sbt("wmh", [128, 4096], BF16)
        wml = sbt("wml", [128, 4096], BF16)
        xx = sbt("xx", [128, 8 * C], F32)
        xfm_h = sbt("xfm_h", [128, NCH * TQ], BF16)
        xfm_l = sbt("xfm_l", [128, NCH * TQ], BF16)
        qh = sbt("qh", [128, NCH * TQ], BF16)
        ql = sbt("ql", [128, NCH * TQ], BF16)
        pool = sbt("pool", [128, 5 * C], F32)
        ssc = sbt("ssc", [128, 768], F32)
        qsc = sbt("qsc", [128, 8 * 128], F32)
        stt = sbt("stt", [128, 48], F32)
        lnw_r = sbt("lnw_r", [128, C], F32)
        lnb_r = sbt("lnb_r", [128, C], F32)
        fb_r = sbt("fb_r", [128, C], F32)
        bv_r = sbt("bv_r", [128, C], F32)
        bqc = sbt("bqc", [128, 16], F32)
        mbig = sbt("mbig", [128, 384], BF16)
        identf = sbt("identf", [128, 128], F32)
        identb = sbt("identb", [128, 128], BF16)
        cst = sbt("cst", [128, 8], F32)
        dwsb = sbt("dwsb", [128, 8 * 32], F32)

        iscr = qsc[:, 0:384].bitcast(I32)

        A = pool[:, 0 * C:1 * C]
        Bv = pool[:, 1 * C:2 * C]
        Cv = pool[:, 2 * C:3 * C]
        Dv = pool[:, 3 * C:4 * C]
        Fv = pool[:, 4 * C:5 * C]
        C_bf = pool[:, 2 * C:3 * C].bitcast(BF16)
        xq_bf = C_bf[:, 0:1024]
        xqfm_bf = C_bf[:, 1024:2048]

        pbs = [es.enter_context(nc.psum_tensor(f"pb{i}", [128, 512], F32)) for i in range(8)]

        sems = {}
        for s in ["sp", "sv", "ss", "sg"] + [f"d{i}" for i in range(8)]:
            sems[s] = es.enter_context(nc.semaphore(s))
        cnt = {s: 0 for s in sems}

        block = es.enter_context(nc.Block())

        def inc(inst, s, k=1):
            cnt[s] += k
            inst.then_inc(sems[s], k)


        class Wt:
            def __init__(self, eng):
                self.eng = eng
                self.seen = {}
            def wait(self, s, v=None):
                v = cnt[s] if v is None else v
                if v > 0 and self.seen.get(s, -1) < v:
                    self.eng.wait_ge(sems[s], v)
                    self.seen[s] = v

        def sec(engine_deco, emit_fn, own=None):
            def _f(eng):
                w = Wt(eng)
                if own is not None:
                    w.wait(own)
                emit_fn(eng, w)
            engine_deco(_f)

        bank_free = [("sp", 0)] * 8

        def pe_wait_bank(w, i):
            s, v = bank_free[i]
            w.wait(s, v)

        def bank_evacuated(i, s):
            bank_free[i] = (s, cnt[s])

        dma_rr = {"ld": 0, "st": 0}
        dma_pools = {"ld": [0, 1, 2, 3, 4, 5], "st": [6, 7]}

        def dma(sync, w, out, in_, pool="ld", **kw):
            ids = dma_pools[pool]
            s = f"d{ids[dma_rr[pool] % len(ids)]}"
            dma_rr[pool] += 1
            w.wait(s, cnt[s])
            i = sync.dma_start(out=out, in_=in_, **kw)
            inc(i, s, 16)
            return (s, cnt[s])

        def wait_toks(w, toks):
            for s, v in toks:
                w.wait(s, v)

        ms = {}
        tok_x = {}
        tok_wm = {}
        tok_wv = {}
        tok_out = {0: None, 1: None, 2: None, 3: None}
        tok_out_all = []
        wm_free_at = {}
        wv_free_at = {}

        def load_set6(se):
            se.add_instruction(mybir.InstLoadActFuncSet(
                name=nc.get_next_instruction_name(), act_func_set_id=6, ins=[], outs=[]))

        tok_const = []

        def e_const_dma(sync, w):
            tok_const.append(dma(sync, w, lnw_r[:], bass.AP(lnw_d.tensor, 0, [[0, 128], [1, C]])))
            tok_const.append(dma(sync, w, lnb_r[:], bass.AP(lnb_d.tensor, 0, [[0, 128], [1, C]])))
            tok_const.append(dma(sync, w, fb_r[:], bass.AP(fb_d.tensor, 0, [[0, 128], [1, C]])))
            tok_const.append(dma(sync, w, bv_r[:], bass.AP(bqkv_d.tensor, 2 * C, [[0, 128], [1, C]])))
            tok_const.append(dma(sync, w, bqc[:], bass.AP(bqkv_d.tensor, 0, [[1, 128], [128, 16]]), allow_slow_non_contiguous=True))
            tok_const.append(dma(sync, w, cst[:, 0:1], bass.AP(da_d.tensor, 0, [[0, 128], [1, 1]])))
            tok_const.append(dma(sync, w, dwsb[:],
                                 bass.AP(dw_d.tensor, 0, [[32, 128], [128 * 32, 8], [1, 32]])))
        sec(block.sync, e_const_dma)

        def e_const_gp(gp, w):
            gp.iota(iscr[:], [[1, 384]], channel_multiplier=-1)
            i = gp.drain()
            inc(i, "sg", 1)
        sec(block.gpsimd, e_const_gp, own="sg")

        def e_const_ve(v, w):
            w.wait("sg")
            wait_toks(w, tok_const)
            v.tensor_single_scalar(Bv[:, 0:384], iscr[:], 0, Alu.is_ge)
            v.tensor_scalar(Fv[:, 0:384], iscr[:], 255, 1.0 / 512.0, Alu.is_le, Alu.mult)
            v.drain()
            v.tensor_tensor(mbig[:], Bv[:, 0:384], Fv[:, 0:384], Alu.mult)
            v.tensor_single_scalar(identf[:], iscr[:, 0:128], 0, Alu.is_equal)
            v.drain()
            v.tensor_copy(identb[:], identf[:])
            v.tensor_single_scalar(cst[:, 1:2], cst[:, 0:1], 4.0, Alu.mult)
            v.memset(cst[:, 7:8], 4.0)
            i = v.drain()
            inc(i, "sv", 1)
        sec(block.vector, e_const_ve, own="sv")

        def e_const_se(s, w):
            w.wait("sv")
            s.activation(cst[:, 2:3], cst[:, 1:2], Act.Tanh)
            i = s.drain()
            inc(i, "ss", 1)
        sec(block.scalar, e_const_se, own="ss")

        def e_const_ve2(v, w):
            w.wait("ss")
            v.tensor_single_scalar(cst[:, 3:4], cst[:, 2:3], 0.5, Alu.mult)
            v.drain()
            v.tensor_single_scalar(cst[:, 4:5], cst[:, 3:4], LN2_HI, Alu.mult)
            v.drain()
            v.scalar_tensor_tensor(cst[:, 5:6], cst[:, 3:4], LN2_LO, cst[:, 4:5], Alu.mult, Alu.add)
            v.drain()
            v.tensor_single_scalar(cst[:, 6:7], cst[:, 5:6], -1.0, Alu.mult)
            i = v.drain()
            inc(i, "sv", 1)
        sec(block.vector, e_const_ve2, own="sv")

        def e_const_se2(s, w):
            w.wait("sv")
            s.activation(cst[:, 0:1], cst[:, 5:6], Act.Exp)
            s.activation(cst[:, 1:2], cst[:, 6:7], Act.Exp)
            i = s.drain()
            inc(i, "ss", 1)
        sec(block.scalar, e_const_se2, own="ss")
        s_act_col = cst[:, 0:1]
        s_inv_col = cst[:, 1:2]

        tok_wst = []
        tok_wst_piece = {}
        tok_ws_ld = [None]
        tok_ws_st_prev = [[]]
        pool_bf = pool[:].bitcast(BF16)

        def emit_wsplit():
            for piece in range(18):
                par = piece % 2
                if piece < 16:
                    m = piece
                    w_src = bass.AP(wqkv_d.tensor, m * 128, [[3 * C, 128], [128 * 3 * C, NCH], [1, 128]])
                    width = 1024
                else:
                    nh = piece - 16
                    w_src = bass.AP(wqkv_d.tensor, 2 * C + nh * 512, [[3 * C, 128], [128 * 3 * C, NCH], [1, 512]])
                    width = 4096
                if piece < 16:
                    scr = xx[:, (4 + par) * C:(4 + par) * C + width]
                    ph = pool_bf[:, par * 1024:par * 1024 + width]
                    pl = pool_bf[:, 4096 + par * 1024:4096 + par * 1024 + width]
                else:
                    scr = xx[:, 4 * C:4 * C + width]
                    ph = pool_bf[:, 0:width]
                    pl = pool_bf[:, 4096:4096 + width]

                def e_ws_dma(sync, w, w_src=w_src, scr=scr, piece=piece):
                    if piece >= 2:
                        w.wait("sv", ms.get(("ws_ve", piece - 1 if piece >= 16 else piece - 2), 0))
                    tok_ws_ld[0] = dma(sync, w, scr.rearrange("p (a b) -> p a b", a=NCH), w_src)
                    ms["ws_ld", piece] = tok_ws_ld[0]
                sec(block.sync, e_ws_dma)

                def e_ws_ve(v, w, scr=scr, ph=ph, pl=pl, piece=piece):
                    prior = []
                    if piece >= 16:
                        prior = ms.get(("ws_st", piece - 1), []) + ms.get(("ws_st", piece - 2), [])
                    else:
                        prior = ms.get(("ws_st", piece - 2), [])
                    wait_toks(w, [ms["ws_ld", piece]] + prior)
                    v.tensor_copy(ph, scr)
                    v.drain()
                    v.tensor_tensor(pl, scr, ph, Alu.subtract)
                    i = v.drain()
                    inc(i, "sv", 1)
                sec(block.vector, e_ws_ve, own="sv")
                ms["ws_ve", piece] = cnt["sv"]

                def e_ws_st(sync, w, piece=piece, ph=ph, pl=pl, width=width):
                    w.wait("sv", ms["ws_ve", piece])
                    if piece < 16:
                        d0 = bass.AP(wqk_d.tensor, piece * 131072, [[1024, 128], [1, 1024]])
                        d1 = bass.AP(wqk_d.tensor, 16 * 131072 + piece * 131072, [[1024, 128], [1, 1024]])
                    else:
                        nh = piece - 16
                        d0 = bass.AP(wv_d.tensor, nh * 524288, [[4096, 128], [1, 4096]])
                        d1 = bass.AP(wv_d.tensor, 2 * 524288 + nh * 524288, [[4096, 128], [1, 4096]])
                    t0_ = dma(sync, w, d0, ph)
                    t1_ = dma(sync, w, d1, pl)
                    ms["ws_st", piece] = [t0_, t1_]
                    tok_wst.extend([t0_, t1_])
                    tok_wst_piece[piece] = [t0_, t1_]
                sec(block.sync, e_ws_st)

        qscr_a = vh[:, 4096:6144].bitcast(F32)
        qscr_c = vl[:, 4096:6144].bitcast(F32)

        def wchunk_ap(chv):
            if chv < 4:
                return pool[:, chv * C:(chv + 1) * C]
            return xx[:, (chv) * C:(chv + 1) * C]

        tok_fw = []
        qa = {n: qsc[:, i * 128:(i + 1) * 128] for i, n in enumerate(
            ["amax", "qt", "u", "g", "p", "tmp", "S", "rS"])}

        def emit_ladder2(v, width):
            a = {k2: ap2[:, 0:width] for k2, ap2 in qa.items()}
            v.memset(a["p"], float(2.0 ** -100))
            v.drain()
            for b in [128, 64, 32, 16, 8, 4, 2, 1]:
                v.tensor_single_scalar(a["tmp"], a["qt"], float(2.0 ** -b), Alu.mult)
                v.drain()
                v.tensor_tensor(a["g"], a["tmp"], a["p"], Alu.is_ge)
                v.drain()
                bb = 64 if b == 128 else b
                v.tensor_scalar(a["g"], a["g"], float(2.0 ** bb - 1), 1.0, Alu.mult, Alu.add)
                v.drain()
                v.tensor_tensor(a["p"], a["p"], a["g"], Alu.mult)
                v.drain()
                if b == 128:
                    v.tensor_tensor(a["p"], a["p"], a["g"], Alu.mult)
                    v.drain()
            v.tensor_single_scalar(a["p"], a["p"], 0.25, Alu.mult)
            v.drain()

        def emit_ladder(v, width):
            a = {k2: ap2[:, 0:width] for k2, ap2 in qa.items()}
            v.tensor_single_scalar(a["u"], a["u"], INV_LN2, Alu.mult)
            v.drain()
            v.tensor_single_scalar(a["u"], a["u"], 102.0, Alu.add)
            v.drain()
            v.memset(a["p"], float(2.0 ** -104))
            for b in [128, 64, 32, 16, 8, 4, 2, 1]:
                v.tensor_single_scalar(a["g"], a["u"], float(b), Alu.is_ge)
                v.drain()
                v.scalar_tensor_tensor(a["u"], a["g"], float(-b), a["u"], Alu.mult, Alu.add)
                bb = 64 if b == 128 else b
                v.tensor_scalar(a["tmp"], a["g"], float(2.0 ** bb - 1), 1.0, Alu.mult, Alu.add)
                v.drain()
                v.tensor_tensor(a["p"], a["p"], a["tmp"], Alu.mult)
                v.drain()
                if b == 128:
                    v.tensor_tensor(a["p"], a["p"], a["tmp"], Alu.mult)
                    v.drain()
            v.tensor_single_scalar(a["tmp"], a["p"], 8.0, Alu.mult)
            v.drain()
            v.tensor_tensor(a["g"], a["qt"], a["tmp"], Alu.is_ge)
            v.drain()
            v.tensor_single_scalar(a["g"], a["g"], 1.0, Alu.add)
            v.drain()
            v.tensor_tensor(a["p"], a["p"], a["g"], Alu.mult)
            v.drain()
            v.tensor_single_scalar(a["tmp"], a["p"], 4.0, Alu.mult)
            v.drain()
            v.tensor_tensor(a["g"], a["qt"], a["tmp"], Alu.is_lt)
            v.drain()
            v.tensor_scalar(a["g"], a["g"], -0.5, 1.0, Alu.mult, Alu.add)
            v.drain()
            v.tensor_tensor(a["p"], a["p"], a["g"], Alu.mult)
            v.drain()


        def emit_fq_dma():
            def e_fq_dma(sync, w):
                wait_toks(w, tok_wst)
                w.wait("sv")
                for chv in range(8):
                    tok_fw.append(dma(sync, w, wchunk_ap(chv), fw_d[chv * 128:(chv + 1) * 128, :]))
            sec(block.sync, e_fq_dma)

        def emit_fq_scales(bi):
            chunks = list(range(bi * 4, bi * 4 + 4))

            def e_fq_amax(v, w, chunks=chunks):
                wait_toks(w, tok_fw)
                w.wait("sp")
                for lc, chv in enumerate(chunks):
                    src2 = wchunk_ap(chv).rearrange("p (a b) -> p a b", b=32)
                    v.tensor_reduce(qa["amax"][:, lc * 32:(lc + 1) * 32].unsqueeze(2),
                                    src2, AX.X, Alu.max, apply_absolute_value=True)
                v.drain()
                v.tensor_single_scalar(qa["qt"], qa["amax"], TINY, Alu.max)
                i = v.drain()
                inc(i, "sv", 1)
            sec(block.vector, e_fq_amax, own="sv")

            def e_fq_se1(s, w, bi=bi):
                w.wait("sv")
                wait_toks(w, tok_const)
                s.activation(qa["u"], qa["qt"], Act.Ln)
                s.activation(qa["S"], dwsb[:, bi * 128:(bi + 1) * 128], Act.Tanh, scale=cst[:, 7:8])
                i = s.drain()
                inc(i, "ss", 1)
            sec(block.scalar, e_fq_se1, own="ss")

            def e_fq_ve2(v, w):
                w.wait("ss")
                emit_ladder(v, 128)
                v.tensor_single_scalar(qa["S"], qa["S"], 0.5, Alu.mult)
                v.drain()
                v.tensor_single_scalar(qa["tmp"], qa["S"], LN2_HI, Alu.mult)
                v.drain()
                v.scalar_tensor_tensor(qa["rS"], qa["S"], LN2_LO, qa["tmp"], Alu.mult, Alu.add)
                i = v.drain()
                inc(i, "sv", 1)
            sec(block.vector, e_fq_ve2, own="sv")

            def e_fq_se2(s, w):
                w.wait("sv")
                s.activation(qa["S"], qa["rS"], Act.Exp)
                i = s.drain()
                inc(i, "ss", 1)
            sec(block.scalar, e_fq_se2, own="ss")

            def e_fq_ve3(v, w):
                w.wait("ss")
                v.tensor_tensor(qa["S"], qa["S"], qa["p"], Alu.mult)
                v.drain()
                v.reciprocal(qa["rS"], qa["S"])
                i = v.drain()
                inc(i, "sv", 1)
            sec(block.vector, e_fq_ve3, own="sv")

        def emit_fq_chunk(lc, chv):
            src2 = wchunk_ap(chv)
            rSb = qa["rS"][:, lc * 32:(lc + 1) * 32].unsqueeze(2).broadcast_to([128, 32, 32])
            Sb = qa["S"][:, lc * 32:(lc + 1) * 32].unsqueeze(2).broadcast_to([128, 32, 32])

            def e_fq_xs(v, w, src2=src2, rSb=rSb):
                w.wait("sv")
                w.wait("ss")
                w.wait("sp")
                v.tensor_tensor(qscr_a.rearrange("p (a b) -> p a b", b=32),
                                src2.rearrange("p (a b) -> p a b", b=32), rSb, Alu.mult)
                i = v.drain()
                inc(i, "sv", 1)
            sec(block.vector, e_fq_xs, own="sv")

            def e_fq_abs(s, w):
                w.wait("sv")
                s.activation(qscr_c, qscr_a, Act.Sign)
                s.drain()
                s.activation(qscr_a, qscr_a, Act.Abs)
                i = s.drain()
                inc(i, "ss", 1)
            sec(block.scalar, e_fq_abs, own="ss")

            hi_dst = bass.AP(kh, 512, [[6144, 128], [768, 8], [1, 128]])
            lo_dst = bass.AP(kh, 640, [[6144, 128], [768, 8], [1, 128]])

            def e_fq_stair(v, w, Sb=Sb, hi_dst=hi_dst, lo_dst=lo_dst):
                w.wait("ss")
                t0, op0, w0 = THRS[0]
                v.tensor_scalar(acc_bf, qscr_a, t0, w0, op0, Alu.mult)
                for t1, op1, w1 in THRS[1:]:
                    v.tensor_scalar(cmp_bf, qscr_a, t1, w1, op1, Alu.mult)
                    v.drain()
                    v.tensor_tensor(acc_bf, acc_bf, cmp_bf, Alu.add)
                    v.drain()
                v.tensor_tensor(qscr_c.rearrange("p (a b) -> p a b", b=32),
                                qscr_c.rearrange("p (a b) -> p a b", b=32), Sb, Alu.mult)
                v.drain()
                v.tensor_tensor(qscr_a, acc_bf, qscr_c, Alu.mult)
                v.drain()
                v.tensor_copy(hi_dst, qscr_a.rearrange("p (a b) -> p a b", b=128))
                v.drain()
                i = v.tensor_tensor(lo_dst, qscr_a.rearrange("p (a b) -> p a b", b=128),
                                    hi_dst, Alu.subtract)
                inc(i, "sv", 1)
            sec(block.vector, e_fq_stair, own="sv")

            for half in range(2):
                def e_fq_tr(pe, w, chv=chv, half=half):
                    w.wait("sv")
                    off = 512 if half == 0 else 640
                    for g in range(2):
                        pe_wait_bank(w, 6 + g)
                        pv = pbs[6 + g][:].bitcast(BF16)
                        for j in range(4):
                            fc = g * 4 + j
                            i = pe.transpose(pv[:, j * 128:(j + 1) * 128],
                                             kh[:, fc * 768 + off: fc * 768 + off + 128], identb[:])
                    inc(i, "sp", 1)
                sec(block.tensor, e_fq_tr, own="sp")

                def e_fq_trev(v, w, chv=chv, half=half):
                    w.wait("sp")
                    dst = wqT_h if half == 0 else wqT_l
                    for g in range(2):
                        pv = pbs[6 + g][:].bitcast(BF16)
                        for j in range(4):
                            fc = g * 4 + j
                            v.tensor_copy(dst[:, fc * C + chv * 128: fc * C + (chv + 1) * 128],
                                          pv[:, j * 128:(j + 1) * 128])
                    i2 = v.drain()
                    inc(i2, "sv", 1)
                sec(block.vector, e_fq_trev, own="sv")
                bank_evacuated(6, "sv")
                bank_evacuated(7, "sv")

        def xslot(t, s):
            return (4 * t + s) % 8

        def xsub(t, s):
            sl = xslot(t, s)
            return xx[:, sl * C:(sl + 1) * C]

        x1sub = xsub

        def region(s):
            return pool[:, s * C:(s + 1) * C]

        def st1(i):
            return stt[:, i:i + 1]

        def st4(i):
            return stt[:, i:i + 4]


        acc_bf = Fv[:].bitcast(BF16)[:, 0:1024]
        cmp_bf = Fv[:].bitcast(BF16)[:, 1024:2048]
        sh_bf = ssc[:, 384:576].bitcast(BF16)
        sl_bf = ssc[:, 576:768].bitcast(BF16)

        def emit_x_p1(t):
            for s in range(NSUB):
                def e_x_load(sync, w, t=t, s=s, otk=tok_out[s], fqv=ms.get("fq_done", 0)):
                    if otk is not None:
                        wait_toks(w, [otk])
                    if t == 1:
                        w.wait("sv", fqv)
                    tok_x[s] = dma(sync, w, xsub(t, s), x_d[t * TQ + s * 128: t * TQ + (s + 1) * 128, :])
                sec(block.sync, e_x_load)

            for s in range(NSUB):
                b0 = (s % 2) * 2
                def e_p1_pe(pe, w, t=t, s=s, b0=b0):
                    wait_toks(w, [tok_x[s]])
                    pe_wait_bank(w, b0)
                    pe_wait_bank(w, b0 + 1)
                    for ci in range(NCH):
                        i = pe.transpose(pbs[b0 + ci // 4][:, (ci % 4) * 128:(ci % 4 + 1) * 128],
                                         xsub(t, s)[:, ci * 128:(ci + 1) * 128],
                                         identf[:])
                    inc(i, "sp", 1)
                sec(block.tensor, e_p1_pe, own="sp")
                ms["p1_tr", t, s] = cnt["sp"]

                def e_p1_ev(se, w, t=t, s=s, b0=b0):
                    w.wait("sp")
                    for g in range(2):
                        dst = xfm_h[:].rearrange("p (a b) -> p a b", b=TQ)[:, g * 4:(g + 1) * 4, s * 128:(s + 1) * 128]
                        i = se.activation(dst, pbs[b0 + g][:].rearrange("p (a b) -> p a b", b=128),
                                          Act.Identity)
                    inc(i, "ss", 1)
                sec(block.scalar, e_p1_ev, own="ss")

                def e_p1_ev2(v, w, t=t, s=s, b0=b0):
                    w.wait("ss")
                    for g in range(2):
                        dh = xfm_h[:].rearrange("p (a b) -> p a b", b=TQ)[:, g * 4:(g + 1) * 4, s * 128:(s + 1) * 128]
                        dl = xfm_l[:].rearrange("p (a b) -> p a b", b=TQ)[:, g * 4:(g + 1) * 4, s * 128:(s + 1) * 128]
                        i = v.tensor_tensor(dl, pbs[b0 + g][:].rearrange("p (a b) -> p a b", b=128),
                                            dh, Alu.subtract)
                    inc(i, "sv", 1)
                sec(block.vector, e_p1_ev2, own="sv")
                bank_evacuated(b0, "sv")
                bank_evacuated(b0 + 1, "sv")

                def e_xlnb(v, w, t=t, s=s, trv=ms["p1_tr", t, s]):
                    w.wait("sp", trv)
                    i = v.tensor_tensor(xsub(t, s), xsub(t, s), lnb_r[:], Alu.add)
                    inc(i, "sv", 1)
                sec(block.vector, e_xlnb, own="sv")
            ms["p1_done", t] = cnt["sv"]

        def emit_qkv_m(t, m):
            if True:
                buf = m % 4
                bank = 4 + (m % 4)

                def e_wm_dma(sync, w, m=m, buf=buf, free_at=wm_free_at.get(buf, 0)):
                    w.wait("sp", free_at)
                    wait_toks(w, tok_wst_piece.get(m, []))
                    t0_ = dma(sync, w,
                              wmh[:, buf * 1024:(buf + 1) * 1024].rearrange("p (a b) -> p a b", b=128),
                              bass.AP(wqk_d.tensor, m * 131072, [[1024, 128], [128, NCH], [1, 128]]))
                    t1_ = dma(sync, w,
                              wml[:, buf * 1024:(buf + 1) * 1024].rearrange("p (a b) -> p a b", b=128),
                              bass.AP(wqk_d.tensor, 16 * 131072 + m * 131072, [[1024, 128], [128, NCH], [1, 128]]))
                    tok_wm[buf] = [t0_, t1_]
                sec(block.sync, e_wm_dma)

                def e_qk_pe(pe, w, m=m, buf=buf, bank=bank, t=t):
                    wait_toks(w, tok_wm[buf])
                    w.wait("sv", ms["p1_done", t])
                    pe_wait_bank(w, bank)
                    for ci in range(NCH):
                        wh_p = wmh[:, buf * 1024 + ci * 128: buf * 1024 + (ci + 1) * 128]
                        wl_p = wml[:, buf * 1024 + ci * 128: buf * 1024 + (ci + 1) * 128]
                        xh_p = xfm_h[:, ci * TQ:(ci + 1) * TQ]
                        xl_p = xfm_l[:, ci * TQ:(ci + 1) * TQ]
                        pe.matmul(pbs[bank][:], wh_p, xh_p, start=(ci == 0), stop=False)
                        pe.matmul(pbs[bank][:], wh_p, xl_p, start=False, stop=False)
                        i = pe.matmul(pbs[bank][:], wl_p, xh_p, start=False, stop=(ci == NCH - 1))
                    inc(i, "sp", 1)
                sec(block.tensor, e_qk_pe, own="sp")
                wm_free_at[buf] = cnt["sp"]

                bqcol = bqc[:, m:m + 1]

                def kpieces(t, ci):
                    base = ci * 768
                    s0 = (4 * t) % 6
                    if s0 <= 2:
                        return [(0, 512, base + s0 * 128)]
                    nfirst = (6 - s0) * 128
                    return [(0, nfirst, base + s0 * 128), (nfirst, 512, base)]

                def e_qk_ev(v, w, m=m, bank=bank, bqcol=bqcol, t=t):
                    w.wait("sp")
                    if m < 8:
                        pieces = [(0, 512, None)]
                    else:
                        pieces = kpieces(t, m - 8)
                    for a, b2, d0 in pieces:
                        if m < 8:
                            dh = qh[:, m * TQ + a: m * TQ + b2]
                            dl = ql[:, m * TQ + a: m * TQ + b2]
                        else:
                            dh = kh[:, d0:d0 + (b2 - a)]
                            dl = kl[:, d0:d0 + (b2 - a)]
                        v.tensor_single_scalar(dh, pbs[bank][:, a:b2], bqcol, Alu.add)
                        v.drain()
                        i = v.scalar_tensor_tensor(dl, pbs[bank][:, a:b2], bqcol, dh, Alu.add, Alu.subtract)
                    inc(i, "sv", 1)
                sec(block.vector, e_qk_ev, own="sv")
                bank_evacuated(bank, "sv")

        def emit_qkv_v(t):
            ms["qk_evac", t] = cnt["sv"]
            for pv_ in range(4):
                nh, cg = pv_ // 2, pv_ % 2
                vbuf = pv_ % 2

                def e_wv_dma(sync, w, nh=nh, cg=cg, vbuf=vbuf, pv_=pv_,
                             free_at=(max(wm_free_at.get(2, 0), wm_free_at.get(3, 0)) if pv_ < 2
                                      else wv_free_at.get(pv_ - 2, 0))):
                    w.wait("sp", free_at)
                    wait_toks(w, tok_wst_piece.get(16 + nh, []))
                    off = nh * 524288 + cg * 2048
                    t0_ = dma(sync, w,
                              wmh[:, vbuf * 2048:(vbuf + 1) * 2048].rearrange("p (a b) -> p a b", b=512),
                              bass.AP(wv_d.tensor, off, [[4096, 128], [512, 4], [1, 512]]))
                    t1_ = dma(sync, w,
                              wml[:, vbuf * 2048:(vbuf + 1) * 2048].rearrange("p (a b) -> p a b", b=512),
                              bass.AP(wv_d.tensor, 2 * 524288 + off, [[4096, 128], [512, 4], [1, 512]]))
                    tok_wv[pv_] = [t0_, t1_]
                sec(block.sync, e_wv_dma)

                for s in range(NSUB):
                    bank = 4 + ((pv_ * NSUB + s) % 4)
                    vslot = (t * NSUB + s) % 6

                    def e_v_pe(pe, w, nh=nh, cg=cg, s=s, bank=bank, t=t, vbuf=vbuf, pv_=pv_):
                        wait_toks(w, tok_wv[pv_])
                        w.wait("sv", ms["p1_done", t])
                        pe_wait_bank(w, bank)
                        for lc in range(4):
                            ci = cg * 4 + lc
                            xh_p = xfm_h[:, ci * TQ + s * 128: ci * TQ + (s + 1) * 128]
                            xl_p = xfm_l[:, ci * TQ + s * 128: ci * TQ + (s + 1) * 128]
                            wh_p = wmh[:, vbuf * 2048 + lc * 512: vbuf * 2048 + (lc + 1) * 512]
                            wl_p = wml[:, vbuf * 2048 + lc * 512: vbuf * 2048 + (lc + 1) * 512]
                            pe.matmul(pbs[bank][:], xh_p, wh_p, start=(cg == 0 and lc == 0), stop=False)
                            pe.matmul(pbs[bank][:], xl_p, wh_p, start=False, stop=False)
                            i = pe.matmul(pbs[bank][:], xh_p, wl_p, start=False,
                                          stop=(cg == 1 and lc == 3))
                        inc(i, "sp", 1)
                    sec(block.tensor, e_v_pe, own="sp")
                    if s == NSUB - 1:
                        wv_free_at[pv_] = cnt["sp"]

                    if cg == 1:
                        def e_v_ev(v, w, nh=nh, s=s, bank=bank, vslot=vslot):
                            w.wait("sp")
                            w.wait("ss")
                            dst_h = vh[:, vslot * C + nh * 512: vslot * C + (nh + 1) * 512]
                            dst_l = vl[:, vslot * C + nh * 512: vslot * C + (nh + 1) * 512]
                            v.tensor_tensor(Fv[:, 0:512], pbs[bank][:], bv_r[:, nh * 512:(nh + 1) * 512], Alu.add)
                            v.drain()
                            v.tensor_copy(dst_h, Fv[:, 0:512])
                            v.drain()
                            i = v.tensor_tensor(dst_l, Fv[:, 0:512], dst_h, Alu.subtract)
                            inc(i, "sv", 1)
                        sec(block.vector, e_v_ev, own="sv")
                        bank_evacuated(bank, "sv")
            ms["v_evac", t] = cnt["sv"]
            for _b in range(4):
                wm_free_at[_b] = max(wm_free_at.get(_b, 0), cnt["sp"])

        def emit_scores_ln(t):
            for s in range(NSUB):
                gt = t * NSUB + s
                tks = [x2 for x2 in [gt - 2, gt - 1, gt] if x2 >= 0]
                scb = s % 2
                aob = [2, 3] if s % 2 == 0 else [6, 7]

                def e_sc_pe(pe, w, t=t, s=s, tks=tks, scb=scb):
                    w.wait("sv", ms["qk_evac", t])
                    pe_wait_bank(w, scb)
                    first = True
                    for i_tk, tkg in enumerate(tks):
                        rb = (tkg % 6) * 128
                        for ci in range(NCH):
                            kh_p = kh[:, ci * 768 + rb: ci * 768 + rb + 128]
                            kl_p = kl[:, ci * 768 + rb: ci * 768 + rb + 128]
                            qh_p = qh[:, ci * TQ + s * 128: ci * TQ + (s + 1) * 128]
                            ql_p = ql[:, ci * TQ + s * 128: ci * TQ + (s + 1) * 128]
                            outp = pbs[scb][:, i_tk * 128:(i_tk + 1) * 128]
                            pe.matmul(outp, kh_p, qh_p, start=first, stop=False)
                            first = False
                            pe.matmul(outp, kh_p, ql_p, start=False, stop=False)
                            last = (i_tk == len(tks) - 1) and (ci == NCH - 1)
                            i = pe.matmul(outp, kl_p, qh_p, start=False, stop=last)
                    inc(i, "sp", 1)
                sec(block.tensor, e_sc_pe, own="sp")

                def e_sc_ev(v, w, t=t, s=s, tks=tks, scb=scb):
                    w.wait("sp")
                    for i_tk, tkg in enumerate(tks):
                        delta = 128 * ((t * NSUB + s) - tkg)
                        v.tensor_tensor(ssc[:, i_tk * 128:(i_tk + 1) * 128],
                                        pbs[scb][:, i_tk * 128:(i_tk + 1) * 128],
                                        mbig[:, delta:delta + 128], Alu.mult)
                    v.drain()
                    n = len(tks) * 128
                    v.tensor_copy(sh_bf[:, 0:n], ssc[:, 0:n])
                    v.drain()
                    i = v.tensor_tensor(sl_bf[:, 0:n], ssc[:, 0:n], sh_bf[:, 0:n], Alu.subtract)
                    inc(i, "sv", 1)
                sec(block.vector, e_sc_ev, own="sv")
                bank_evacuated(scb, "sv")
                ms["sc_split", t, s] = cnt["sv"]

                def e_ao_pe(pe, w, t=t, s=s, tks=tks, aob=aob):
                    w.wait("sv", max(ms["sc_split", t, s], ms["v_evac", t]))
                    for h in range(2):
                        bank = aob[h]
                        pe_wait_bank(w, bank)
                        first = True
                        for i_tk, tkg in enumerate(tks):
                            vslot = tkg % 6
                            sh_p = sh_bf[:, i_tk * 128:(i_tk + 1) * 128]
                            sl_p = sl_bf[:, i_tk * 128:(i_tk + 1) * 128]
                            vh_p = vh[:, vslot * C + h * 512: vslot * C + (h + 1) * 512]
                            vl_p = vl[:, vslot * C + h * 512: vslot * C + (h + 1) * 512]
                            pe.matmul(pbs[bank][:], sh_p, vh_p, start=first, stop=False)
                            first = False
                            pe.matmul(pbs[bank][:], sh_p, vl_p, start=False, stop=False)
                            i = pe.matmul(pbs[bank][:], sl_p, vh_p, start=False,
                                          stop=(i_tk == len(tks) - 1))
                    inc(i, "sp", 1)
                sec(block.tensor, e_ao_pe, own="sp")

                def e_p4a_se(se, w, s=s, aob=aob, t=t):
                    w.wait("sp")
                    if t == 0:
                        w.wait("sv", ms.get("fq_done", 0))
                    if tok_out[s] is not None:
                        wait_toks(w, [tok_out[s]])
                    se.activation(region(s)[:, 0:512], pbs[aob[0]][:], Act.Identity,
                                  accum_out=st1(0 + s))
                    se.activation(region(s)[:, 512:1024], pbs[aob[1]][:], Act.Identity,
                                  accum_out=st1(4 + s))
                    se.drain()
                    se.activation(Fv[:, 0:512], region(s)[:, 0:512], Act.Square, accum_out=st1(8 + s))
                    i = se.activation(Fv[:, 512:1024], region(s)[:, 512:1024], Act.Square,
                                      accum_out=st1(12 + s))
                    inc(i, "ss", 1)
                sec(block.scalar, e_p4a_se, own="ss")
                bank_evacuated(aob[0], "ss")
                bank_evacuated(aob[1], "ss")

            def e_p4b_ve(v, w):
                w.wait("ss")
                v.tensor_tensor(st4(16), st4(0), st4(4), Alu.add)
                v.tensor_tensor(st4(20), st4(8), st4(12), Alu.add)
                v.drain()
                v.tensor_single_scalar(st4(16), st4(16), 1.0 / 1024.0, Alu.mult)
                v.tensor_single_scalar(st4(20), st4(20), 1.0 / 1024.0, Alu.mult)
                v.drain()
                v.tensor_tensor(st4(24), st4(16), st4(16), Alu.mult)
                v.drain()
                v.tensor_tensor(st4(24), st4(20), st4(24), Alu.subtract)
                v.drain()
                i = v.tensor_single_scalar(st4(24), st4(24), EPS, Alu.add)
                inc(i, "sv", 1)
            sec(block.vector, e_p4b_ve, own="sv")

            def e_p4b_se(se, w):
                w.wait("sv")
                i = se.activation(st4(24), st4(24), Act.Sqrt)
                inc(i, "ss", 1)
            sec(block.scalar, e_p4b_se, own="ss")

            def e_p4b_ve2(v, w):
                w.wait("ss")
                v.reciprocal(st4(20), st4(24))
                v.drain()
                v.tensor_tensor(st4(24), st4(16), st4(20), Alu.mult)
                v.drain()
                i = v.tensor_single_scalar(st4(24), st4(24), -1.0, Alu.mult)
                inc(i, "sv", 1)
            sec(block.vector, e_p4b_ve2, own="sv")

            for s in range(NSUB):
                def e_p4c_ve(v, w, s=s, t=t):
                    w.wait("ss")
                    v.tensor_scalar(Fv[:], region(s)[:], st1(20 + s), st1(24 + s), Alu.mult, Alu.add)
                    v.drain()
                    v.tensor_tensor(Fv[:], Fv[:], lnw_r[:], Alu.mult)
                    v.drain()
                    v.tensor_tensor(region(s)[:], Fv[:], xsub(t, s), Alu.add)
                    v.drain()
                    v.tensor_reduce(st1(0 + s), region(s)[:], AX.X, Alu.add)
                    v.tensor_tensor(Fv[:], region(s)[:], region(s)[:], Alu.mult)
                    v.drain()
                    i = v.tensor_reduce(st1(4 + s), Fv[:], AX.X, Alu.add)
                    inc(i, "sv", 1)
                sec(block.vector, e_p4c_ve, own="sv")

            def e_p4d_ve(v, w):
                w.wait("sv")
                v.tensor_single_scalar(st4(0), st4(0), 1.0 / 1024.0, Alu.mult)
                v.tensor_single_scalar(st4(4), st4(4), 1.0 / 1024.0, Alu.mult)
                v.drain()
                v.tensor_tensor(st4(8), st4(0), st4(0), Alu.mult)
                v.drain()
                v.tensor_tensor(st4(8), st4(4), st4(8), Alu.subtract)
                v.drain()
                i = v.tensor_single_scalar(st4(8), st4(8), EPS, Alu.add)
                inc(i, "sv", 1)
            sec(block.vector, e_p4d_ve, own="sv")

            def e_p4d_se(se, w):
                w.wait("sv")
                i = se.activation(st4(8), st4(8), Act.Sqrt)
                inc(i, "ss", 1)
            sec(block.scalar, e_p4d_se, own="ss")

            def e_p4d_ve2(v, w):
                w.wait("ss")
                v.reciprocal(st4(28), st4(8))
                v.drain()
                v.tensor_tensor(st4(32), st4(0), st4(28), Alu.mult)
                v.drain()
                i = v.tensor_single_scalar(st4(32), st4(32), -1.0, Alu.mult)
                inc(i, "sv", 1)
            sec(block.vector, e_p4d_ve2, own="sv")

            for s in range(NSUB):
                def e_p4e_ve(v, w, s=s, t=t):
                    w.wait("ss")
                    v.tensor_scalar(x1sub(t, s), region(s)[:], st1(28 + s), st1(32 + s), Alu.mult, Alu.add)
                    v.drain()
                    i = v.tensor_reduce(qa["amax"][:, s * 32:(s + 1) * 32].unsqueeze(2),
                                        x1sub(t, s).rearrange("p (a b) -> p a b", b=32),
                                        AX.X, Alu.max, apply_absolute_value=True)
                    inc(i, "sv", 1)
                sec(block.vector, e_p4e_ve, own="sv")

            def e_as_ve2(v, w):
                w.wait("sv")
                w.wait("ss")
                v.tensor_single_scalar(qa["qt"][:, 0:128], qa["amax"][:, 0:128], TINY, Alu.max)
                v.drain()
                emit_ladder2(v, 128)
                v.reciprocal(qa["rS"][:, 0:128], qa["p"][:, 0:128])
                v.drain()
                i = v.tensor_single_scalar(qa["rS"][:, 0:128], qa["rS"][:, 0:128], s_inv_col, Alu.mult)
                inc(i, "sv", 1)
            sec(block.vector, e_as_ve2, own="sv")

        def emit_quant_sub(t, s):
            if True:
                rSb = qa["rS"][:, s * 32:(s + 1) * 32].unsqueeze(2).broadcast_to([128, 32, 32])
                pb_ = qa["p"][:, s * 32:(s + 1) * 32].unsqueeze(2).broadcast_to([128, 32, 32])
                xqtb = [4, 5] if s % 2 == 0 else [6, 7]
                ffnb = [4, 5] if s % 2 == 0 else [6, 7]
                R_xs = region((s + 1) % 4)
                R_sg = region((s + 2) % 4)
                R_xq = region(s).bitcast(BF16)

                def e_q_stair(v, w, s=s, rSb=rSb, pb_=pb_, R_xs=R_xs, R_sg=R_sg, R_xq=R_xq, t=t):
                    w.wait("sv")
                    w.wait("ss")
                    tmp = region(s)
                    v.tensor_tensor(R_xs[:].rearrange("p (a b) -> p a b", b=32),
                                    x1sub(t, s).rearrange("p (a b) -> p a b", b=32), rSb, Alu.mult)
                    v.drain()
                    v.tensor_single_scalar(tmp[:], R_xs[:], -1.0, Alu.mult)
                    v.tensor_single_scalar(acc_bf, R_xs[:], 0.0, Alu.is_gt)
                    v.tensor_single_scalar(cmp_bf, R_xs[:], 0.0, Alu.is_lt)
                    v.drain()
                    v.tensor_tensor(R_sg[:], R_xs[:], tmp[:], Alu.max)
                    v.drain()
                    v.tensor_tensor(tmp[:], acc_bf, cmp_bf, Alu.subtract)
                    v.drain()
                    t0, op0, w0 = THRS[0]
                    v.tensor_scalar(acc_bf, R_sg[:], t0, w0, op0, Alu.mult)
                    for t1, op1, w1 in THRS[1:]:
                        v.tensor_scalar(cmp_bf, R_sg[:], t1, w1, op1, Alu.mult)
                        v.drain()
                        v.tensor_tensor(acc_bf, acc_bf, cmp_bf, Alu.add)
                        v.drain()
                    v.tensor_tensor(R_xs[:].rearrange("p (a b) -> p a b", b=32),
                                    tmp[:].rearrange("p (a b) -> p a b", b=32), pb_, Alu.mult)
                    v.drain()
                    i = v.tensor_tensor(R_xq[:, 0:1024], acc_bf, R_xs[:], Alu.mult)
                    inc(i, "sv", 1)
                sec(block.vector, e_q_stair, own="sv")

                def e_xqt_pe(pe, w, s=s, xqtb=xqtb, R_xq=R_xq):
                    w.wait("sv")
                    for g in range(2):
                        pe_wait_bank(w, xqtb[g])
                        pv = pbs[xqtb[g]][:].bitcast(BF16)
                        for j in range(4):
                            fc = g * 4 + j
                            i = pe.transpose(pv[:, j * 128:(j + 1) * 128],
                                             R_xq[:, fc * 128:(fc + 1) * 128], identb[:])
                    inc(i, "sp", 1)
                sec(block.tensor, e_xqt_pe, own="sp")

                def e_xqt_ev(v, w, s=s, xqtb=xqtb, R_xq=R_xq):
                    w.wait("sp")
                    v.tensor_copy(R_xq[:, 1024:1536], pbs[xqtb[0]][:].bitcast(BF16)[:, 0:512])
                    i = v.tensor_copy(R_xq[:, 1536:2048], pbs[xqtb[1]][:].bitcast(BF16)[:, 0:512])
                    inc(i, "sv", 1)
                sec(block.vector, e_xqt_ev, own="sv")
                bank_evacuated(xqtb[0], "sv")
                bank_evacuated(xqtb[1], "sv")

                def e_ffn_pe(pe, w, s=s, ffnb=ffnb, R_xq=R_xq):
                    w.wait("sv")
                    for h in range(2):
                        bank = ffnb[h]
                        pe_wait_bank(w, bank)
                        for fc in range(NCH):
                            xq_p = R_xq[:, 1024 + fc * 128:1024 + (fc + 1) * 128]
                            wh_p = wqT_h[:, fc * C + h * 512: fc * C + (h + 1) * 512]
                            wl_p = wqT_l[:, fc * C + h * 512: fc * C + (h + 1) * 512]
                            pe.matmul(pbs[bank][:], xq_p, wh_p, start=(fc == 0), stop=False)
                            i = pe.matmul(pbs[bank][:], xq_p, wl_p, start=False, stop=(fc == NCH - 1))
                    inc(i, "sp", 1)
                sec(block.tensor, e_ffn_pe, own="sp")

                def e_ffn_ev(v, w, s=s, ffnb=ffnb, R_sg=R_sg, t=t):
                    w.wait("sp")
                    for h in range(2):
                        v.scalar_tensor_tensor(R_sg[:, h * 512:(h + 1) * 512], pbs[ffnb[h]][:],
                                               s_act_col, fb_r[:, h * 512:(h + 1) * 512],
                                               Alu.mult, Alu.add)
                    v.drain()
                    v.tensor_tensor(x1sub(t, s), R_sg[:], x1sub(t, s), Alu.add)
                    v.drain()
                    i = v.tensor_reduce(st1(0 + s), x1sub(t, s), AX.X, Alu.add)
                    inc(i, "sv", 1)
                sec(block.vector, e_ffn_ev, own="sv")
                bank_evacuated(ffnb[0], "sv")
                bank_evacuated(ffnb[1], "sv")

                def e_f_se1(se, w, s=s, R_xs=R_xs, t=t):
                    w.wait("sv")
                    i = se.activation(R_xs[:], x1sub(t, s), Act.Square, accum_out=st1(4 + s))
                    inc(i, "ss", 1)
                sec(block.scalar, e_f_se1, own="ss")

        def emit_out(t):
            def e_p8_ve(v, w):
                w.wait("sv")
                w.wait("ss")
                v.tensor_single_scalar(st4(0), st4(0), 1.0 / 1024.0, Alu.mult)
                v.tensor_single_scalar(st4(4), st4(4), 1.0 / 1024.0, Alu.mult)
                v.drain()
                v.tensor_tensor(st4(8), st4(0), st4(0), Alu.mult)
                v.drain()
                v.tensor_tensor(st4(8), st4(4), st4(8), Alu.subtract)
                v.drain()
                i = v.tensor_single_scalar(st4(8), st4(8), EPS, Alu.add)
                inc(i, "sv", 1)
            sec(block.vector, e_p8_ve, own="sv")

            def e_p8_se(se, w):
                w.wait("sv")
                i = se.activation(st4(8), st4(8), Act.Sqrt)
                inc(i, "ss", 1)
            sec(block.scalar, e_p8_se, own="ss")

            def e_p8_ve2(v, w):
                w.wait("ss")
                v.reciprocal(st4(36), st4(8))
                v.drain()
                v.tensor_tensor(st4(40), st4(0), st4(36), Alu.mult)
                v.drain()
                i = v.tensor_single_scalar(st4(40), st4(40), -1.0, Alu.mult)
                inc(i, "sv", 1)
            sec(block.vector, e_p8_ve2, own="sv")

            for s in range(NSUB):
                def e_out_se(se, w, s=s, t=t):
                    w.wait("sv")
                    if tok_out[s] is not None:
                        wait_toks(w, [tok_out[s]])
                    i = se.activation(region(s)[:], x1sub(t, s), Act.Identity,
                                      bias=st1(40 + s), scale=st1(36 + s))
                    inc(i, "ss", 1)
                    w.wait("ss", cnt["ss"])
                    tk_ = dma(se, w, out_d[t * TQ + s * 128: t * TQ + (s + 1) * 128, :], region(s)[:], pool="st")
                    tok_out[s] = tk_
                    tok_out_all.append(tk_)
                sec(block.scalar, e_out_se, own="ss")

        emit_x_p1(0)
        emit_wsplit()
        emit_fq_dma()
        for bi in range(2):
            emit_fq_scales(bi)
            for lc in range(4):
                chv = bi * 4 + lc
                emit_fq_chunk(lc, chv)
                emit_qkv_m(0, 2 * chv)
                emit_qkv_m(0, 2 * chv + 1)
        ms["fq_done"] = cnt["sv"]
        emit_qkv_v(0)
        for t in range(NT):
            emit_scores_ln(t)
            if t < NT - 1:
                emit_x_p1(t + 1)
                for s in range(NSUB):
                    for m in range(4 * s, 4 * s + 4):
                        emit_qkv_m(t + 1, m)
                    emit_quant_sub(t, s)
                emit_qkv_v(t + 1)
            else:
                for s in range(NSUB):
                    emit_quant_sub(t, s)
            emit_out(t)

        def e_fin(sync, w):
            wait_toks(w, tok_out_all)
        sec(block.sync, e_fin)

    return nc


def kernel(L0_attn_x, L0_attn_W_qkv, L0_attn_b_qkv, L0_attn_ln_w, L0_attn_ln_b,
           L0_ffn_weight, L0_ffn_bias, L0_ffn_delta_w, L0_ffn_delta_act):
    x = np.ascontiguousarray(np.asarray(L0_attn_x, dtype=np.float32))
    B, T, _ = x.shape
    nc = build_nc(T)
    base = {
        "w_qkv": np.ascontiguousarray(L0_attn_W_qkv, dtype=np.float32),
        "b_qkv": np.ascontiguousarray(L0_attn_b_qkv, dtype=np.float32),
        "ln_w": np.ascontiguousarray(L0_attn_ln_w, dtype=np.float32),
        "ln_b": np.ascontiguousarray(L0_attn_ln_b, dtype=np.float32),
        "f_w": np.ascontiguousarray(L0_ffn_weight, dtype=np.float32),
        "f_b": np.ascontiguousarray(L0_ffn_bias, dtype=np.float32),
        "d_w": np.ascontiguousarray(L0_ffn_delta_w, dtype=np.float32),
        "d_a": np.ascontiguousarray(L0_ffn_delta_act, dtype=np.float32),
    }
    in_maps = [dict(base, x=x[b]) for b in range(B)]
    res = run_bass_kernel_spmd(nc, in_maps, core_ids=list(range(B)))
    out = np.stack([np.asarray(res.results[b]["out"]) for b in range(B)], axis=0)
    return out.astype(np.float32)


# version 24
# speedup vs baseline: 1.1113x; 1.1113x over previous
import numpy as np
import concourse.bass as bass
import concourse.mybir as mybir
from concourse.bass_utils import run_bass_kernel_spmd

F32 = mybir.dt.float32
BF16 = mybir.dt.bfloat16
I32 = mybir.dt.int32
Alu = mybir.AluOpType
Act = mybir.ActivationFunctionType
AX = mybir.AxisListType

C = 1024
TQ = 512
SUB = 128
NSUB = TQ // SUB
NCH = C // 128
INV_LN2 = float(1.0 / np.log(2.0))
LN2_HI = float(np.float32(np.log(2.0)))
LN2_LO = float(np.float64(np.log(2.0)) - np.float64(np.float32(np.log(2.0))))
EPS = 1e-5
TINY = 1e-30
THRS = [(0.25, Alu.is_gt, 0.5), (0.75, Alu.is_ge, 0.5), (1.25, Alu.is_gt, 0.5),
        (1.75, Alu.is_ge, 0.5), (2.5, Alu.is_gt, 1.0), (3.5, Alu.is_ge, 1.0),
        (5.0, Alu.is_gt, 2.0)]


def build_nc(T):
    NT = T // TQ
    nc = bass.Bass("TRN2", target_bir_lowering=False)

    x_d = nc.dram_tensor("x", [T, C], F32, kind="ExternalInput").ap()
    wqkv_d = nc.dram_tensor("w_qkv", [C, 3 * C], F32, kind="ExternalInput").ap()
    bqkv_d = nc.dram_tensor("b_qkv", [3 * C], F32, kind="ExternalInput").ap()
    lnw_d = nc.dram_tensor("ln_w", [C], F32, kind="ExternalInput").ap()
    lnb_d = nc.dram_tensor("ln_b", [C], F32, kind="ExternalInput").ap()
    fw_d = nc.dram_tensor("f_w", [C, C], F32, kind="ExternalInput").ap()
    fb_d = nc.dram_tensor("f_b", [C], F32, kind="ExternalInput").ap()
    dw_d = nc.dram_tensor("d_w", [C, 32], F32, kind="ExternalInput").ap()
    da_d = nc.dram_tensor("d_a", [1], F32, kind="ExternalInput").ap()
    out_d = nc.dram_tensor("out", [T, C], F32, kind="ExternalOutput").ap()
    wqk_d = nc.dram_tensor("wqk", [2, 16, 128, NCH, 128], BF16, kind="Internal").ap()
    wv_d = nc.dram_tensor("wvs", [2, 2, 128, NCH, 512], BF16, kind="Internal").ap()

    import contextlib
    with contextlib.ExitStack() as es:
        def sbt(name, shape, dt):
            return es.enter_context(nc.sbuf_tensor(name, shape, dt))

        wqT_h = sbt("wqT_h", [128, 8 * C], BF16)
        wqT_l = sbt("wqT_l", [128, 8 * C], BF16)
        kh = sbt("kh", [128, NCH * 768], BF16)
        kl = sbt("kl", [128, NCH * 768], BF16)
        vh = sbt("vh", [128, 6 * C], BF16)
        vl = sbt("vl", [128, 6 * C], BF16)
        wmh = sbt("wmh", [128, 4096], BF16)
        wml = sbt("wml", [128, 4096], BF16)
        xx = sbt("xx", [128, 8 * C], F32)
        xfm_h = sbt("xfm_h", [128, NCH * TQ], BF16)
        xfm_l = sbt("xfm_l", [128, NCH * TQ], BF16)
        qh = sbt("qh", [128, NCH * TQ], BF16)
        ql = sbt("ql", [128, NCH * TQ], BF16)
        pool = sbt("pool", [128, 5 * C], F32)
        ssc = sbt("ssc", [128, 768], F32)
        qsc = sbt("qsc", [128, 8 * 128], F32)
        stt = sbt("stt", [128, 48], F32)
        lnw_r = sbt("lnw_r", [128, C], F32)
        lnb_r = sbt("lnb_r", [128, C], F32)
        fb_r = sbt("fb_r", [128, C], F32)
        bv_r = sbt("bv_r", [128, C], F32)
        bqc = sbt("bqc", [128, 16], F32)
        mbig = sbt("mbig", [128, 384], BF16)
        identf = sbt("identf", [128, 128], F32)
        identb = sbt("identb", [128, 128], BF16)
        cst = sbt("cst", [128, 8], F32)
        dwsb = sbt("dwsb", [128, 8 * 32], F32)

        iscr = qsc[:, 0:384].bitcast(I32)

        A = pool[:, 0 * C:1 * C]
        Bv = pool[:, 1 * C:2 * C]
        Cv = pool[:, 2 * C:3 * C]
        Dv = pool[:, 3 * C:4 * C]
        Fv = pool[:, 4 * C:5 * C]
        C_bf = pool[:, 2 * C:3 * C].bitcast(BF16)
        xq_bf = C_bf[:, 0:1024]
        xqfm_bf = C_bf[:, 1024:2048]

        pbs = [es.enter_context(nc.psum_tensor(f"pb{i}", [128, 512], F32)) for i in range(8)]

        sems = {}
        for s in ["sp", "sv", "ss", "sg"] + [f"d{i}" for i in range(8)]:
            sems[s] = es.enter_context(nc.semaphore(s))
        cnt = {s: 0 for s in sems}

        block = es.enter_context(nc.Block())

        def inc(inst, s, k=1):
            cnt[s] += k
            inst.then_inc(sems[s], k)


        class Wt:
            def __init__(self, eng):
                self.eng = eng
                self.seen = {}
            def wait(self, s, v=None):
                v = cnt[s] if v is None else v
                if v > 0 and self.seen.get(s, -1) < v:
                    self.eng.wait_ge(sems[s], v)
                    self.seen[s] = v

        def sec(engine_deco, emit_fn, own=None):
            def _f(eng):
                w = Wt(eng)
                if own is not None:
                    w.wait(own)
                emit_fn(eng, w)
            engine_deco(_f)

        bank_free = [("sp", 0)] * 8

        def pe_wait_bank(w, i):
            s, v = bank_free[i]
            w.wait(s, v)

        def bank_evacuated(i, s):
            bank_free[i] = (s, cnt[s])

        dma_rr = {"ld": 0, "st": 0}
        dma_pools = {"ld": [0, 1, 2, 3, 4, 5], "st": [6, 7]}

        def dma(sync, w, out, in_, pool="ld", **kw):
            ids = dma_pools[pool]
            s = f"d{ids[dma_rr[pool] % len(ids)]}"
            dma_rr[pool] += 1
            w.wait(s, cnt[s])
            i = sync.dma_start(out=out, in_=in_, **kw)
            inc(i, s, 16)
            return (s, cnt[s])

        def wait_toks(w, toks):
            for s, v in toks:
                w.wait(s, v)

        ms = {}
        tok_x = {}
        tok_wm = {}
        tok_wv = {}
        tok_out = {0: None, 1: None, 2: None, 3: None}
        tok_out_all = []
        wm_free_at = {}
        wv_free_at = {}

        def load_set6(se):
            se.add_instruction(mybir.InstLoadActFuncSet(
                name=nc.get_next_instruction_name(), act_func_set_id=6, ins=[], outs=[]))

        tok_const = []

        def e_const_dma(sync, w):
            tok_const.append(dma(sync, w, lnw_r[:], bass.AP(lnw_d.tensor, 0, [[0, 128], [1, C]])))
            tok_const.append(dma(sync, w, lnb_r[:], bass.AP(lnb_d.tensor, 0, [[0, 128], [1, C]])))
            tok_const.append(dma(sync, w, fb_r[:], bass.AP(fb_d.tensor, 0, [[0, 128], [1, C]])))
            tok_const.append(dma(sync, w, bv_r[:], bass.AP(bqkv_d.tensor, 2 * C, [[0, 128], [1, C]])))
            tok_const.append(dma(sync, w, bqc[:], bass.AP(bqkv_d.tensor, 0, [[1, 128], [128, 16]]), allow_slow_non_contiguous=True))
            tok_const.append(dma(sync, w, cst[:, 0:1], bass.AP(da_d.tensor, 0, [[0, 128], [1, 1]])))
            tok_const.append(dma(sync, w, dwsb[:],
                                 bass.AP(dw_d.tensor, 0, [[32, 128], [128 * 32, 8], [1, 32]])))
        sec(block.sync, e_const_dma)

        def e_const_gp(gp, w):
            gp.iota(iscr[:], [[1, 384]], channel_multiplier=-1)
            i = gp.drain()
            inc(i, "sg", 1)
        sec(block.gpsimd, e_const_gp, own="sg")

        def e_const_ve(v, w):
            w.wait("sg")
            wait_toks(w, tok_const)
            v.tensor_single_scalar(Bv[:, 0:384], iscr[:], 0, Alu.is_ge)
            v.tensor_scalar(Fv[:, 0:384], iscr[:], 255, 1.0 / 512.0, Alu.is_le, Alu.mult)
            v.drain()
            v.tensor_tensor(mbig[:], Bv[:, 0:384], Fv[:, 0:384], Alu.mult)
            v.tensor_single_scalar(identf[:], iscr[:, 0:128], 0, Alu.is_equal)
            v.drain()
            v.tensor_copy(identb[:], identf[:])
            v.tensor_single_scalar(cst[:, 1:2], cst[:, 0:1], 4.0, Alu.mult)
            v.memset(cst[:, 7:8], 4.0)
            i = v.drain()
            inc(i, "sv", 1)
        sec(block.vector, e_const_ve, own="sv")

        def e_const_se(s, w):
            w.wait("sv")
            s.activation(cst[:, 2:3], cst[:, 1:2], Act.Tanh)
            i = s.drain()
            inc(i, "ss", 1)
        sec(block.scalar, e_const_se, own="ss")

        def e_const_ve2(v, w):
            w.wait("ss")
            v.tensor_single_scalar(cst[:, 3:4], cst[:, 2:3], 0.5, Alu.mult)
            v.drain()
            v.tensor_single_scalar(cst[:, 4:5], cst[:, 3:4], LN2_HI, Alu.mult)
            v.drain()
            v.scalar_tensor_tensor(cst[:, 5:6], cst[:, 3:4], LN2_LO, cst[:, 4:5], Alu.mult, Alu.add)
            v.drain()
            v.tensor_single_scalar(cst[:, 6:7], cst[:, 5:6], -1.0, Alu.mult)
            i = v.drain()
            inc(i, "sv", 1)
        sec(block.vector, e_const_ve2, own="sv")

        def e_const_se2(s, w):
            w.wait("sv")
            s.activation(cst[:, 0:1], cst[:, 5:6], Act.Exp)
            s.activation(cst[:, 1:2], cst[:, 6:7], Act.Exp)
            i = s.drain()
            inc(i, "ss", 1)
        sec(block.scalar, e_const_se2, own="ss")
        s_act_col = cst[:, 0:1]
        s_inv_col = cst[:, 1:2]

        tok_wst = []
        tok_wst_piece = {}
        tok_ws_ld = [None]
        tok_ws_st_prev = [[]]
        pool_bf = pool[:].bitcast(BF16)

        def emit_wsplit():
            for piece in range(18):
                par = piece % 2
                if piece < 16:
                    m = piece
                    w_src = bass.AP(wqkv_d.tensor, m * 128, [[3 * C, 128], [128 * 3 * C, NCH], [1, 128]])
                    width = 1024
                else:
                    nh = piece - 16
                    w_src = bass.AP(wqkv_d.tensor, 2 * C + nh * 512, [[3 * C, 128], [128 * 3 * C, NCH], [1, 512]])
                    width = 4096
                if piece < 16:
                    scr = xx[:, (4 + par) * C:(4 + par) * C + width]
                    ph = pool_bf[:, par * 1024:par * 1024 + width]
                    pl = pool_bf[:, 4096 + par * 1024:4096 + par * 1024 + width]
                else:
                    scr = xx[:, 4 * C:4 * C + width]
                    ph = pool_bf[:, 0:width]
                    pl = pool_bf[:, 4096:4096 + width]

                def e_ws_dma(sync, w, w_src=w_src, scr=scr, piece=piece):
                    if piece >= 2:
                        w.wait("sv", ms.get(("ws_ve", piece - 1 if piece >= 16 else piece - 2), 0))
                    tok_ws_ld[0] = dma(sync, w, scr.rearrange("p (a b) -> p a b", a=NCH), w_src)
                    ms["ws_ld", piece] = tok_ws_ld[0]
                sec(block.sync, e_ws_dma)

                def e_ws_ve(v, w, scr=scr, ph=ph, pl=pl, piece=piece):
                    prior = []
                    if piece >= 16:
                        prior = ms.get(("ws_st", piece - 1), []) + ms.get(("ws_st", piece - 2), [])
                    else:
                        prior = ms.get(("ws_st", piece - 2), [])
                    wait_toks(w, [ms["ws_ld", piece]] + prior)
                    v.tensor_copy(ph, scr)
                    v.drain()
                    v.tensor_tensor(pl, scr, ph, Alu.subtract)
                    i = v.drain()
                    inc(i, "sv", 1)
                sec(block.vector, e_ws_ve, own="sv")
                ms["ws_ve", piece] = cnt["sv"]

                def e_ws_st(sync, w, piece=piece, ph=ph, pl=pl, width=width):
                    w.wait("sv", ms["ws_ve", piece])
                    if piece < 16:
                        d0 = bass.AP(wqk_d.tensor, piece * 131072, [[1024, 128], [1, 1024]])
                        d1 = bass.AP(wqk_d.tensor, 16 * 131072 + piece * 131072, [[1024, 128], [1, 1024]])
                    else:
                        nh = piece - 16
                        d0 = bass.AP(wv_d.tensor, nh * 524288, [[4096, 128], [1, 4096]])
                        d1 = bass.AP(wv_d.tensor, 2 * 524288 + nh * 524288, [[4096, 128], [1, 4096]])
                    t0_ = dma(sync, w, d0, ph)
                    t1_ = dma(sync, w, d1, pl)
                    ms["ws_st", piece] = [t0_, t1_]
                    tok_wst.extend([t0_, t1_])
                    tok_wst_piece[piece] = [t0_, t1_]
                sec(block.sync, e_ws_st)

        qscr_a = vh[:, 4096:6144].bitcast(F32)
        qscr_c = vl[:, 4096:6144].bitcast(F32)

        def wchunk_ap(chv):
            if chv < 4:
                return pool[:, chv * C:(chv + 1) * C]
            return xx[:, (chv) * C:(chv + 1) * C]

        tok_fw = []
        qa = {n: qsc[:, i * 128:(i + 1) * 128] for i, n in enumerate(
            ["amax", "qt", "u", "g", "p", "tmp", "S", "rS"])}

        def emit_ladder2(v, width):
            a = {k2: ap2[:, 0:width] for k2, ap2 in qa.items()}
            v.memset(a["p"], float(2.0 ** -100))
            v.drain()
            for b in [128, 64, 32, 16, 8, 4, 2, 1]:
                v.tensor_single_scalar(a["tmp"], a["qt"], float(2.0 ** -b), Alu.mult)
                v.drain()
                v.tensor_tensor(a["g"], a["tmp"], a["p"], Alu.is_ge)
                v.drain()
                bb = 64 if b == 128 else b
                v.tensor_scalar(a["g"], a["g"], float(2.0 ** bb - 1), 1.0, Alu.mult, Alu.add)
                v.drain()
                v.tensor_tensor(a["p"], a["p"], a["g"], Alu.mult)
                v.drain()
                if b == 128:
                    v.tensor_tensor(a["p"], a["p"], a["g"], Alu.mult)
                    v.drain()
            v.tensor_single_scalar(a["p"], a["p"], 0.25, Alu.mult)
            v.drain()

        def emit_ladder(v, width):
            a = {k2: ap2[:, 0:width] for k2, ap2 in qa.items()}
            v.tensor_single_scalar(a["u"], a["u"], INV_LN2, Alu.mult)
            v.drain()
            v.tensor_single_scalar(a["u"], a["u"], 102.0, Alu.add)
            v.drain()
            v.memset(a["p"], float(2.0 ** -104))
            for b in [128, 64, 32, 16, 8, 4, 2, 1]:
                v.tensor_single_scalar(a["g"], a["u"], float(b), Alu.is_ge)
                v.drain()
                v.scalar_tensor_tensor(a["u"], a["g"], float(-b), a["u"], Alu.mult, Alu.add)
                bb = 64 if b == 128 else b
                v.tensor_scalar(a["tmp"], a["g"], float(2.0 ** bb - 1), 1.0, Alu.mult, Alu.add)
                v.drain()
                v.tensor_tensor(a["p"], a["p"], a["tmp"], Alu.mult)
                v.drain()
                if b == 128:
                    v.tensor_tensor(a["p"], a["p"], a["tmp"], Alu.mult)
                    v.drain()
            v.tensor_single_scalar(a["tmp"], a["p"], 8.0, Alu.mult)
            v.drain()
            v.tensor_tensor(a["g"], a["qt"], a["tmp"], Alu.is_ge)
            v.drain()
            v.tensor_single_scalar(a["g"], a["g"], 1.0, Alu.add)
            v.drain()
            v.tensor_tensor(a["p"], a["p"], a["g"], Alu.mult)
            v.drain()
            v.tensor_single_scalar(a["tmp"], a["p"], 4.0, Alu.mult)
            v.drain()
            v.tensor_tensor(a["g"], a["qt"], a["tmp"], Alu.is_lt)
            v.drain()
            v.tensor_scalar(a["g"], a["g"], -0.5, 1.0, Alu.mult, Alu.add)
            v.drain()
            v.tensor_tensor(a["p"], a["p"], a["g"], Alu.mult)
            v.drain()


        def emit_fq_dma():
            def e_fq_dma(sync, w):
                wait_toks(w, tok_wst)
                w.wait("sv")
                for chv in range(8):
                    tok_fw.append(dma(sync, w, wchunk_ap(chv), fw_d[chv * 128:(chv + 1) * 128, :]))
            sec(block.sync, e_fq_dma)

        def emit_fq_scales(bi):
            chunks = list(range(bi * 4, bi * 4 + 4))

            def e_fq_amax(v, w, chunks=chunks):
                wait_toks(w, tok_fw)
                w.wait("sp")
                for lc, chv in enumerate(chunks):
                    src2 = wchunk_ap(chv).rearrange("p (a b) -> p a b", b=32)
                    v.tensor_reduce(qa["amax"][:, lc * 32:(lc + 1) * 32].unsqueeze(2),
                                    src2, AX.X, Alu.max, apply_absolute_value=True)
                v.drain()
                v.tensor_single_scalar(qa["qt"], qa["amax"], TINY, Alu.max)
                i = v.drain()
                inc(i, "sv", 1)
            sec(block.vector, e_fq_amax, own="sv")

            def e_fq_se1(s, w, bi=bi):
                w.wait("sv")
                wait_toks(w, tok_const)
                s.activation(qa["u"], qa["qt"], Act.Ln)
                s.activation(qa["S"], dwsb[:, bi * 128:(bi + 1) * 128], Act.Tanh, scale=cst[:, 7:8])
                i = s.drain()
                inc(i, "ss", 1)
            sec(block.scalar, e_fq_se1, own="ss")

            def e_fq_ve2(v, w):
                w.wait("ss")
                emit_ladder(v, 128)
                v.tensor_single_scalar(qa["S"], qa["S"], 0.5, Alu.mult)
                v.drain()
                v.tensor_single_scalar(qa["tmp"], qa["S"], LN2_HI, Alu.mult)
                v.drain()
                v.scalar_tensor_tensor(qa["rS"], qa["S"], LN2_LO, qa["tmp"], Alu.mult, Alu.add)
                i = v.drain()
                inc(i, "sv", 1)
            sec(block.vector, e_fq_ve2, own="sv")

            def e_fq_se2(s, w):
                w.wait("sv")
                s.activation(qa["S"], qa["rS"], Act.Exp)
                i = s.drain()
                inc(i, "ss", 1)
            sec(block.scalar, e_fq_se2, own="ss")

            def e_fq_ve3(v, w):
                w.wait("ss")
                v.tensor_tensor(qa["S"], qa["S"], qa["p"], Alu.mult)
                v.drain()
                v.reciprocal(qa["rS"], qa["S"])
                i = v.drain()
                inc(i, "sv", 1)
            sec(block.vector, e_fq_ve3, own="sv")

        def emit_fq_chunk(lc, chv):
            src2 = wchunk_ap(chv)
            rSb = qa["rS"][:, lc * 32:(lc + 1) * 32].unsqueeze(2).broadcast_to([128, 32, 32])
            Sb = qa["S"][:, lc * 32:(lc + 1) * 32].unsqueeze(2).broadcast_to([128, 32, 32])

            def e_fq_xs(v, w, src2=src2, rSb=rSb):
                w.wait("sv")
                w.wait("ss")
                w.wait("sp")
                v.tensor_tensor(qscr_a.rearrange("p (a b) -> p a b", b=32),
                                src2.rearrange("p (a b) -> p a b", b=32), rSb, Alu.mult)
                i = v.drain()
                inc(i, "sv", 1)
            sec(block.vector, e_fq_xs, own="sv")

            def e_fq_abs(s, w):
                w.wait("sv")
                s.activation(qscr_c, qscr_a, Act.Sign)
                s.drain()
                s.activation(qscr_a, qscr_a, Act.Abs)
                i = s.drain()
                inc(i, "ss", 1)
            sec(block.scalar, e_fq_abs, own="ss")

            hi_dst = bass.AP(kh, 512, [[6144, 128], [768, 8], [1, 128]])
            lo_dst = bass.AP(kh, 640, [[6144, 128], [768, 8], [1, 128]])

            def e_fq_stair(v, w, Sb=Sb, hi_dst=hi_dst, lo_dst=lo_dst):
                w.wait("ss")
                t0, op0, w0 = THRS[0]
                v.tensor_scalar(acc_bf, qscr_a, t0, w0, op0, Alu.mult)
                for t1, op1, w1 in THRS[1:]:
                    v.tensor_scalar(cmp_bf, qscr_a, t1, w1, op1, Alu.mult)
                    v.drain()
                    v.tensor_tensor(acc_bf, acc_bf, cmp_bf, Alu.add)
                    v.drain()
                v.tensor_tensor(qscr_c.rearrange("p (a b) -> p a b", b=32),
                                qscr_c.rearrange("p (a b) -> p a b", b=32), Sb, Alu.mult)
                v.drain()
                v.tensor_tensor(qscr_a, acc_bf, qscr_c, Alu.mult)
                v.drain()
                v.tensor_copy(hi_dst, qscr_a.rearrange("p (a b) -> p a b", b=128))
                v.drain()
                i = v.tensor_tensor(lo_dst, qscr_a.rearrange("p (a b) -> p a b", b=128),
                                    hi_dst, Alu.subtract)
                inc(i, "sv", 1)
            sec(block.vector, e_fq_stair, own="sv")

            for half in range(2):
                def e_fq_tr(pe, w, chv=chv, half=half):
                    w.wait("sv")
                    off = 512 if half == 0 else 640
                    for g in range(2):
                        pe_wait_bank(w, 6 + g)
                        pv = pbs[6 + g][:].bitcast(BF16)
                        for j in range(4):
                            fc = g * 4 + j
                            i = pe.transpose(pv[:, j * 128:(j + 1) * 128],
                                             kh[:, fc * 768 + off: fc * 768 + off + 128], identb[:])
                    inc(i, "sp", 1)
                sec(block.tensor, e_fq_tr, own="sp")

                def e_fq_trev(v, w, chv=chv, half=half):
                    w.wait("sp")
                    dst = wqT_h if half == 0 else wqT_l
                    for g in range(2):
                        pv = pbs[6 + g][:].bitcast(BF16)
                        for j in range(4):
                            fc = g * 4 + j
                            v.tensor_copy(dst[:, fc * C + chv * 128: fc * C + (chv + 1) * 128],
                                          pv[:, j * 128:(j + 1) * 128])
                    i2 = v.drain()
                    inc(i2, "sv", 1)
                sec(block.vector, e_fq_trev, own="sv")
                bank_evacuated(6, "sv")
                bank_evacuated(7, "sv")

        def xslot(t, s):
            return (4 * t + s) % 8

        def xsub(t, s):
            sl = xslot(t, s)
            return xx[:, sl * C:(sl + 1) * C]

        x1sub = xsub

        def region(s):
            return pool[:, s * C:(s + 1) * C]

        def st1(i):
            return stt[:, i:i + 1]

        def st4(i):
            return stt[:, i:i + 4]

        pair_m = stt[:, 0:8].rearrange("p (a b) -> p a b", b=2)[:, :, 0:1]
        pair_v = stt[:, 0:8].rearrange("p (a b) -> p a b", b=2)[:, :, 1:2]


        acc_bf = Fv[:].bitcast(BF16)[:, 0:1024]
        cmp_bf = Fv[:].bitcast(BF16)[:, 1024:2048]
        sh_bf = ssc[:, 384:576].bitcast(BF16)
        sl_bf = ssc[:, 576:768].bitcast(BF16)

        def emit_x_p1(t):
            for s in range(NSUB):
                def e_x_load(sync, w, t=t, s=s, otk=tok_out[s], fqv=ms.get("fq_done", 0)):
                    if otk is not None:
                        wait_toks(w, [otk])
                    if t == 1:
                        w.wait("sv", fqv)
                    tok_x[s] = dma(sync, w, xsub(t, s), x_d[t * TQ + s * 128: t * TQ + (s + 1) * 128, :])
                sec(block.sync, e_x_load)

            for s in range(NSUB):
                b0 = (s % 2) * 2
                def e_p1_pe(pe, w, t=t, s=s, b0=b0):
                    wait_toks(w, [tok_x[s]])
                    pe_wait_bank(w, b0)
                    pe_wait_bank(w, b0 + 1)
                    for ci in range(NCH):
                        i = pe.transpose(pbs[b0 + ci // 4][:, (ci % 4) * 128:(ci % 4 + 1) * 128],
                                         xsub(t, s)[:, ci * 128:(ci + 1) * 128],
                                         identf[:])
                    inc(i, "sp", 1)
                sec(block.tensor, e_p1_pe, own="sp")
                ms["p1_tr", t, s] = cnt["sp"]

                def e_p1_ev(se, w, t=t, s=s, b0=b0):
                    w.wait("sp")
                    for g in range(2):
                        dst = xfm_h[:].rearrange("p (a b) -> p a b", b=TQ)[:, g * 4:(g + 1) * 4, s * 128:(s + 1) * 128]
                        i = se.activation(dst, pbs[b0 + g][:].rearrange("p (a b) -> p a b", b=128),
                                          Act.Identity)
                    inc(i, "ss", 1)
                sec(block.scalar, e_p1_ev, own="ss")

                def e_p1_ev2(v, w, t=t, s=s, b0=b0):
                    w.wait("ss")
                    for g in range(2):
                        dh = xfm_h[:].rearrange("p (a b) -> p a b", b=TQ)[:, g * 4:(g + 1) * 4, s * 128:(s + 1) * 128]
                        dl = xfm_l[:].rearrange("p (a b) -> p a b", b=TQ)[:, g * 4:(g + 1) * 4, s * 128:(s + 1) * 128]
                        i = v.tensor_tensor(dl, pbs[b0 + g][:].rearrange("p (a b) -> p a b", b=128),
                                            dh, Alu.subtract)
                    inc(i, "sv", 1)
                sec(block.vector, e_p1_ev2, own="sv")
                bank_evacuated(b0, "sv")
                bank_evacuated(b0 + 1, "sv")

                def e_xlnb(v, w, t=t, s=s, trv=ms["p1_tr", t, s]):
                    w.wait("sp", trv)
                    i = v.tensor_tensor(xsub(t, s), xsub(t, s), lnb_r[:], Alu.add)
                    inc(i, "sv", 1)
                sec(block.vector, e_xlnb, own="sv")
            ms["p1_done", t] = cnt["sv"]

        def emit_qkv_m(t, m, defer_ev=False):
            if True:
                buf = m % 4
                bank = 4 + (m % 4)

                def e_wm_dma(sync, w, m=m, buf=buf, free_at=wm_free_at.get(buf, 0)):
                    w.wait("sp", free_at)
                    wait_toks(w, tok_wst_piece.get(m, []))
                    t0_ = dma(sync, w,
                              wmh[:, buf * 1024:(buf + 1) * 1024].rearrange("p (a b) -> p a b", b=128),
                              bass.AP(wqk_d.tensor, m * 131072, [[1024, 128], [128, NCH], [1, 128]]))
                    t1_ = dma(sync, w,
                              wml[:, buf * 1024:(buf + 1) * 1024].rearrange("p (a b) -> p a b", b=128),
                              bass.AP(wqk_d.tensor, 16 * 131072 + m * 131072, [[1024, 128], [128, NCH], [1, 128]]))
                    tok_wm[buf] = [t0_, t1_]
                sec(block.sync, e_wm_dma)

                def e_qk_pe(pe, w, m=m, buf=buf, bank=bank, t=t):
                    wait_toks(w, tok_wm[buf])
                    w.wait("sv", ms["p1_done", t])
                    pe_wait_bank(w, bank)
                    for ci in range(NCH):
                        wh_p = wmh[:, buf * 1024 + ci * 128: buf * 1024 + (ci + 1) * 128]
                        wl_p = wml[:, buf * 1024 + ci * 128: buf * 1024 + (ci + 1) * 128]
                        xh_p = xfm_h[:, ci * TQ:(ci + 1) * TQ]
                        xl_p = xfm_l[:, ci * TQ:(ci + 1) * TQ]
                        pe.matmul(pbs[bank][:], wh_p, xh_p, start=(ci == 0), stop=False)
                        pe.matmul(pbs[bank][:], wh_p, xl_p, start=False, stop=False)
                        i = pe.matmul(pbs[bank][:], wl_p, xh_p, start=False, stop=(ci == NCH - 1))
                    inc(i, "sp", 1)
                sec(block.tensor, e_qk_pe, own="sp")
                wm_free_at[buf] = cnt["sp"]

                bqcol = bqc[:, m:m + 1]

                def kpieces(t, ci):
                    base = ci * 768
                    s0 = (4 * t) % 6
                    if s0 <= 2:
                        return [(0, 512, base + s0 * 128)]
                    nfirst = (6 - s0) * 128
                    return [(0, nfirst, base + s0 * 128), (nfirst, 512, base)]

                sp_at = cnt["sp"]

                def mk_ev(m=m, bank=bank, bqcol=bqcol, t=t, sp_at=sp_at):
                    def e_qk_ev(v, w, m=m, bank=bank, bqcol=bqcol, t=t, sp_at=sp_at):
                        w.wait("sp")
                        if m < 8:
                            pieces = [(0, 512, None)]
                        else:
                            pieces = kpieces(t, m - 8)
                        for a, b2, d0 in pieces:
                            if m < 8:
                                dh = qh[:, m * TQ + a: m * TQ + b2]
                                dl = ql[:, m * TQ + a: m * TQ + b2]
                            else:
                                dh = kh[:, d0:d0 + (b2 - a)]
                                dl = kl[:, d0:d0 + (b2 - a)]
                            v.tensor_single_scalar(dh, pbs[bank][:, a:b2], bqcol, Alu.add)
                            v.drain()
                            i = v.scalar_tensor_tensor(dl, pbs[bank][:, a:b2], bqcol, dh, Alu.add, Alu.subtract)
                        inc(i, "sv", 1)
                    sec(block.vector, e_qk_ev, own="sv")
                    bank_evacuated(bank, "sv")
                if defer_ev:
                    return mk_ev
                mk_ev()

        def emit_qkv_v(t):
            ms["qk_evac", t] = cnt["sv"]
            for pv_ in range(4):
                nh, cg = pv_ // 2, pv_ % 2
                vbuf = pv_ % 2

                def e_wv_dma(sync, w, nh=nh, cg=cg, vbuf=vbuf, pv_=pv_,
                             free_at=(max(wm_free_at.get(2, 0), wm_free_at.get(3, 0)) if pv_ < 2
                                      else wv_free_at.get(pv_ - 2, 0))):
                    w.wait("sp", free_at)
                    wait_toks(w, tok_wst_piece.get(16 + nh, []))
                    off = nh * 524288 + cg * 2048
                    t0_ = dma(sync, w,
                              wmh[:, vbuf * 2048:(vbuf + 1) * 2048].rearrange("p (a b) -> p a b", b=512),
                              bass.AP(wv_d.tensor, off, [[4096, 128], [512, 4], [1, 512]]))
                    t1_ = dma(sync, w,
                              wml[:, vbuf * 2048:(vbuf + 1) * 2048].rearrange("p (a b) -> p a b", b=512),
                              bass.AP(wv_d.tensor, 2 * 524288 + off, [[4096, 128], [512, 4], [1, 512]]))
                    tok_wv[pv_] = [t0_, t1_]
                sec(block.sync, e_wv_dma)

                for s in range(NSUB):
                    bank = 4 + ((pv_ * NSUB + s) % 4)
                    vslot = (t * NSUB + s) % 6

                    def e_v_pe(pe, w, nh=nh, cg=cg, s=s, bank=bank, t=t, vbuf=vbuf, pv_=pv_):
                        wait_toks(w, tok_wv[pv_])
                        w.wait("sv", ms["p1_done", t])
                        pe_wait_bank(w, bank)
                        for lc in range(4):
                            ci = cg * 4 + lc
                            xh_p = xfm_h[:, ci * TQ + s * 128: ci * TQ + (s + 1) * 128]
                            xl_p = xfm_l[:, ci * TQ + s * 128: ci * TQ + (s + 1) * 128]
                            wh_p = wmh[:, vbuf * 2048 + lc * 512: vbuf * 2048 + (lc + 1) * 512]
                            wl_p = wml[:, vbuf * 2048 + lc * 512: vbuf * 2048 + (lc + 1) * 512]
                            pe.matmul(pbs[bank][:], xh_p, wh_p, start=(cg == 0 and lc == 0), stop=False)
                            pe.matmul(pbs[bank][:], xl_p, wh_p, start=False, stop=False)
                            i = pe.matmul(pbs[bank][:], xh_p, wl_p, start=False,
                                          stop=(cg == 1 and lc == 3))
                        inc(i, "sp", 1)
                    sec(block.tensor, e_v_pe, own="sp")
                    if s == NSUB - 1:
                        wv_free_at[pv_] = cnt["sp"]

                    if cg == 1:
                        def e_v_ev(v, w, nh=nh, s=s, bank=bank, vslot=vslot):
                            w.wait("sp")
                            w.wait("ss")
                            dst_h = vh[:, vslot * C + nh * 512: vslot * C + (nh + 1) * 512]
                            dst_l = vl[:, vslot * C + nh * 512: vslot * C + (nh + 1) * 512]
                            v.tensor_tensor(Fv[:, 0:512], pbs[bank][:], bv_r[:, nh * 512:(nh + 1) * 512], Alu.add)
                            v.drain()
                            v.tensor_copy(dst_h, Fv[:, 0:512])
                            v.drain()
                            i = v.tensor_tensor(dst_l, Fv[:, 0:512], dst_h, Alu.subtract)
                            inc(i, "sv", 1)
                        sec(block.vector, e_v_ev, own="sv")
                        bank_evacuated(bank, "sv")
            ms["v_evac", t] = cnt["sv"]
            for _b in range(4):
                wm_free_at[_b] = max(wm_free_at.get(_b, 0), cnt["sp"])

        def emit_scores_ln(t):
            for s in range(NSUB):
                gt = t * NSUB + s
                tks = [x2 for x2 in [gt - 2, gt - 1, gt] if x2 >= 0]
                scb = s % 2
                aob = [2, 3] if s % 2 == 0 else [6, 7]

                def e_sc_pe(pe, w, t=t, s=s, tks=tks, scb=scb):
                    w.wait("sv", ms["qk_evac", t])
                    pe_wait_bank(w, scb)
                    first = True
                    for i_tk, tkg in enumerate(tks):
                        rb = (tkg % 6) * 128
                        for ci in range(NCH):
                            kh_p = kh[:, ci * 768 + rb: ci * 768 + rb + 128]
                            kl_p = kl[:, ci * 768 + rb: ci * 768 + rb + 128]
                            qh_p = qh[:, ci * TQ + s * 128: ci * TQ + (s + 1) * 128]
                            ql_p = ql[:, ci * TQ + s * 128: ci * TQ + (s + 1) * 128]
                            outp = pbs[scb][:, i_tk * 128:(i_tk + 1) * 128]
                            pe.matmul(outp, kh_p, qh_p, start=first, stop=False)
                            first = False
                            pe.matmul(outp, kh_p, ql_p, start=False, stop=False)
                            last = (i_tk == len(tks) - 1) and (ci == NCH - 1)
                            i = pe.matmul(outp, kl_p, qh_p, start=False, stop=last)
                    inc(i, "sp", 1)
                sec(block.tensor, e_sc_pe, own="sp")

                def e_sc_ev(v, w, t=t, s=s, tks=tks, scb=scb):
                    w.wait("sp")
                    for i_tk, tkg in enumerate(tks):
                        delta = 128 * ((t * NSUB + s) - tkg)
                        v.tensor_tensor(ssc[:, i_tk * 128:(i_tk + 1) * 128],
                                        pbs[scb][:, i_tk * 128:(i_tk + 1) * 128],
                                        mbig[:, delta:delta + 128], Alu.mult)
                    v.drain()
                    n = len(tks) * 128
                    v.tensor_copy(sh_bf[:, 0:n], ssc[:, 0:n])
                    v.drain()
                    i = v.tensor_tensor(sl_bf[:, 0:n], ssc[:, 0:n], sh_bf[:, 0:n], Alu.subtract)
                    inc(i, "sv", 1)
                sec(block.vector, e_sc_ev, own="sv")
                bank_evacuated(scb, "sv")
                ms["sc_split", t, s] = cnt["sv"]

                def e_ao_pe(pe, w, t=t, s=s, tks=tks, aob=aob):
                    w.wait("sv", max(ms["sc_split", t, s], ms["v_evac", t]))
                    for h in range(2):
                        bank = aob[h]
                        pe_wait_bank(w, bank)
                        first = True
                        for i_tk, tkg in enumerate(tks):
                            vslot = tkg % 6
                            sh_p = sh_bf[:, i_tk * 128:(i_tk + 1) * 128]
                            sl_p = sl_bf[:, i_tk * 128:(i_tk + 1) * 128]
                            vh_p = vh[:, vslot * C + h * 512: vslot * C + (h + 1) * 512]
                            vl_p = vl[:, vslot * C + h * 512: vslot * C + (h + 1) * 512]
                            pe.matmul(pbs[bank][:], sh_p, vh_p, start=first, stop=False)
                            first = False
                            pe.matmul(pbs[bank][:], sh_p, vl_p, start=False, stop=False)
                            i = pe.matmul(pbs[bank][:], sl_p, vh_p, start=False,
                                          stop=(i_tk == len(tks) - 1))
                    inc(i, "sp", 1)
                sec(block.tensor, e_ao_pe, own="sp")

                def e_p4a_se(se, w, s=s, aob=aob, t=t):
                    w.wait("sp")
                    if t == 0:
                        w.wait("sv", ms.get("fq_done", 0))
                    if tok_out[s] is not None:
                        wait_toks(w, [tok_out[s]])
                    se.activation(region(s)[:, 0:512], pbs[aob[0]][:], Act.Identity)
                    i = se.activation(region(s)[:, 512:1024], pbs[aob[1]][:], Act.Identity)
                    inc(i, "ss", 1)
                sec(block.scalar, e_p4a_se, own="ss")
                bank_evacuated(aob[0], "ss")
                bank_evacuated(aob[1], "ss")

                def e_p4a_ve(v, w, s=s):
                    w.wait("ss")
                    bscr = qa["S"][:, 16 * s:16 * s + 12]
                    v.bn_stats(bscr[:, 0:6], region(s)[:, 0:512])
                    v.bn_stats(bscr[:, 6:12], region(s)[:, 512:1024])
                    v.drain()
                    i = v.bn_aggr(stt[:, 2 * s:2 * s + 2], bscr)
                    inc(i, "sv", 1)
                sec(block.vector, e_p4a_ve, own="sv")

            def e_p4b_ve(v, w):
                w.wait("sv")
                i = v.tensor_single_scalar(st4(24).unsqueeze(2), pair_v, EPS, Alu.add)
                inc(i, "sv", 1)
            sec(block.vector, e_p4b_ve, own="sv")

            def e_p4b_se(se, w):
                w.wait("sv")
                i = se.activation(st4(24), st4(24), Act.Sqrt)
                inc(i, "ss", 1)
            sec(block.scalar, e_p4b_se, own="ss")

            def e_p4b_ve2(v, w):
                w.wait("ss")
                v.reciprocal(st4(20), st4(24))
                v.drain()
                v.tensor_tensor(st4(24).unsqueeze(2), pair_m, st4(20).unsqueeze(2), Alu.mult)
                v.drain()
                i = v.tensor_single_scalar(st4(24), st4(24), -1.0, Alu.mult)
                inc(i, "sv", 1)
            sec(block.vector, e_p4b_ve2, own="sv")

            for s in range(NSUB):
                def e_p4c_ve(v, w, s=s, t=t):
                    w.wait("ss")
                    v.tensor_scalar(Fv[:], region(s)[:], st1(20 + s), st1(24 + s), Alu.mult, Alu.add)
                    v.drain()
                    v.tensor_tensor(Fv[:], Fv[:], lnw_r[:], Alu.mult)
                    v.drain()
                    v.tensor_tensor(region(s)[:], Fv[:], xsub(t, s), Alu.add)
                    v.drain()
                    bscr = qa["S"][:, 16 * s:16 * s + 12]
                    v.bn_stats(bscr[:, 0:6], region(s)[:, 0:512])
                    v.bn_stats(bscr[:, 6:12], region(s)[:, 512:1024])
                    v.drain()
                    i = v.bn_aggr(stt[:, 2 * s:2 * s + 2], bscr)
                    inc(i, "sv", 1)
                sec(block.vector, e_p4c_ve, own="sv")

            def e_p4d_ve(v, w):
                w.wait("sv")
                i = v.tensor_single_scalar(st4(8).unsqueeze(2), pair_v, EPS, Alu.add)
                inc(i, "sv", 1)
            sec(block.vector, e_p4d_ve, own="sv")

            def e_p4d_se(se, w):
                w.wait("sv")
                i = se.activation(st4(8), st4(8), Act.Sqrt)
                inc(i, "ss", 1)
            sec(block.scalar, e_p4d_se, own="ss")

            def e_p4d_ve2(v, w):
                w.wait("ss")
                v.reciprocal(st4(28), st4(8))
                v.drain()
                v.tensor_tensor(st4(32).unsqueeze(2), pair_m, st4(28).unsqueeze(2), Alu.mult)
                v.drain()
                i = v.tensor_single_scalar(st4(32), st4(32), -1.0, Alu.mult)
                inc(i, "sv", 1)
            sec(block.vector, e_p4d_ve2, own="sv")

            for s in range(NSUB):
                def e_p4e_ve(v, w, s=s, t=t):
                    w.wait("ss")
                    v.tensor_scalar(x1sub(t, s), region(s)[:], st1(28 + s), st1(32 + s), Alu.mult, Alu.add)
                    v.drain()
                    i = v.tensor_reduce(qa["amax"][:, s * 32:(s + 1) * 32].unsqueeze(2),
                                        x1sub(t, s).rearrange("p (a b) -> p a b", b=32),
                                        AX.X, Alu.max, apply_absolute_value=True)
                    inc(i, "sv", 1)
                sec(block.vector, e_p4e_ve, own="sv")
                if s == 2:
                    emit_q_prelude(t, 0)

            def e_as_ve2(v, w):
                w.wait("sv")
                w.wait("ss")
                v.tensor_single_scalar(qa["qt"][:, 0:128], qa["amax"][:, 0:128], TINY, Alu.max)
                v.drain()
                emit_ladder2(v, 128)
                v.reciprocal(qa["rS"][:, 0:128], qa["p"][:, 0:128])
                v.drain()
                i = v.tensor_single_scalar(qa["rS"][:, 0:128], qa["rS"][:, 0:128], s_inv_col, Alu.mult)
                inc(i, "sv", 1)
            sec(block.vector, e_as_ve2, own="sv")

        def emit_q_prelude(t, s):
            R_xs = region((s + 1) % 4)
            R_sg = region((s + 2) % 4)
            sgn = region(s).bitcast(BF16)[:, 1024:2048]

            def e_q_pre(v, w, s=s, t=t, R_xs=R_xs, R_sg=R_sg, sgn=sgn):
                w.wait("sv")
                w.wait("ss")
                v.tensor_scalar(sgn, x1sub(t, s), 0.0, -0.5, Alu.is_gt, Alu.add)
                v.tensor_single_scalar(R_xs[:], x1sub(t, s), -1.0, Alu.mult)
                v.drain()
                i = v.tensor_tensor(R_sg[:], x1sub(t, s), R_xs[:], Alu.max)
                inc(i, "sv", 1)
            sec(block.vector, e_q_pre, own="sv")

        def emit_quant_sub(t, s, pre_evs=()):
            if True:
                rSb = qa["rS"][:, s * 32:(s + 1) * 32].unsqueeze(2).broadcast_to([128, 32, 32])
                pb_ = qa["p"][:, s * 32:(s + 1) * 32].unsqueeze(2).broadcast_to([128, 32, 32])
                xqtb = [4, 5] if s % 2 == 0 else [6, 7]
                ffnb = [4, 5] if s % 2 == 0 else [6, 7]
                R_xs = region((s + 1) % 4)
                R_sg = region((s + 2) % 4)
                R_xq = region(s).bitcast(BF16)
                sgn = region(s).bitcast(BF16)[:, 1024:2048]
                if s != 0:
                    emit_q_prelude(t, s)

                def e_q_stair(v, w, s=s, rSb=rSb, pb_=pb_, R_xs=R_xs, R_sg=R_sg, R_xq=R_xq, sgn=sgn, t=t):
                    w.wait("sv")
                    w.wait("ss")
                    v.tensor_tensor(R_xs[:].rearrange("p (a b) -> p a b", b=32),
                                    R_sg[:].rearrange("p (a b) -> p a b", b=32), rSb, Alu.mult)
                    v.drain()
                    v.tensor_scalar(acc_bf, R_xs[:], 5.0, 1.0, Alu.is_gt, Alu.mult)
                    v.drain()
                    for tr, op, dst, src in ((5.0, Alu.is_gt, cmp_bf, acc_bf),
                                             (3.5, Alu.is_ge, acc_bf, cmp_bf),
                                             (2.5, Alu.is_gt, cmp_bf, acc_bf)):
                        v.scalar_tensor_tensor(dst, R_xs[:], tr, src, op, Alu.add)
                        v.drain()
                    v.tensor_tensor(acc_bf, cmp_bf, cmp_bf, Alu.add)
                    v.drain()
                    for tr, op, dst, src in ((1.75, Alu.is_ge, cmp_bf, acc_bf),
                                             (1.25, Alu.is_gt, acc_bf, cmp_bf),
                                             (0.75, Alu.is_ge, cmp_bf, acc_bf),
                                             (0.25, Alu.is_gt, acc_bf, cmp_bf)):
                        v.scalar_tensor_tensor(dst, R_xs[:], tr, src, op, Alu.add)
                        v.drain()
                    v.tensor_tensor(cmp_bf, acc_bf, sgn, Alu.mult)
                    v.drain()
                    i = v.tensor_tensor(R_xq[:, 0:1024].rearrange("p (a b) -> p a b", b=32),
                                        cmp_bf.rearrange("p (a b) -> p a b", b=32), pb_, Alu.mult)
                    inc(i, "sv", 1)
                sec(block.vector, e_q_stair, own="sv")
                for ev_ in pre_evs:
                    ev_()

                def e_xqt_pe(pe, w, s=s, xqtb=xqtb, R_xq=R_xq):
                    w.wait("sv")
                    for g in range(2):
                        pe_wait_bank(w, xqtb[g])
                        pv = pbs[xqtb[g]][:].bitcast(BF16)
                        for j in range(4):
                            fc = g * 4 + j
                            i = pe.transpose(pv[:, j * 128:(j + 1) * 128],
                                             R_xq[:, fc * 128:(fc + 1) * 128], identb[:])
                    inc(i, "sp", 1)
                sec(block.tensor, e_xqt_pe, own="sp")

                def e_xqt_ev(v, w, s=s, xqtb=xqtb, R_xq=R_xq):
                    w.wait("sp")
                    v.tensor_copy(R_xq[:, 1024:1536], pbs[xqtb[0]][:].bitcast(BF16)[:, 0:512])
                    i = v.tensor_copy(R_xq[:, 1536:2048], pbs[xqtb[1]][:].bitcast(BF16)[:, 0:512])
                    inc(i, "sv", 1)
                sec(block.vector, e_xqt_ev, own="sv")
                bank_evacuated(xqtb[0], "sv")
                bank_evacuated(xqtb[1], "sv")

                def e_ffn_pe(pe, w, s=s, ffnb=ffnb, R_xq=R_xq):
                    w.wait("sv")
                    for h in range(2):
                        bank = ffnb[h]
                        pe_wait_bank(w, bank)
                        for fc in range(NCH):
                            xq_p = R_xq[:, 1024 + fc * 128:1024 + (fc + 1) * 128]
                            wh_p = wqT_h[:, fc * C + h * 512: fc * C + (h + 1) * 512]
                            wl_p = wqT_l[:, fc * C + h * 512: fc * C + (h + 1) * 512]
                            pe.matmul(pbs[bank][:], xq_p, wh_p, start=(fc == 0), stop=False)
                            i = pe.matmul(pbs[bank][:], xq_p, wl_p, start=False, stop=(fc == NCH - 1))
                    inc(i, "sp", 1)
                sec(block.tensor, e_ffn_pe, own="sp")

                def e_ffn_ev(v, w, s=s, ffnb=ffnb, R_sg=R_sg, t=t):
                    w.wait("sp")
                    for h in range(2):
                        v.scalar_tensor_tensor(R_sg[:, h * 512:(h + 1) * 512], pbs[ffnb[h]][:],
                                               s_act_col, fb_r[:, h * 512:(h + 1) * 512],
                                               Alu.mult, Alu.add)
                    v.drain()
                    v.tensor_tensor(x1sub(t, s), R_sg[:], x1sub(t, s), Alu.add)
                    v.drain()
                    bscr = qa["S"][:, 16 * s:16 * s + 12]
                    v.bn_stats(bscr[:, 0:6], x1sub(t, s)[:, 0:512])
                    v.bn_stats(bscr[:, 6:12], x1sub(t, s)[:, 512:1024])
                    v.drain()
                    i = v.bn_aggr(stt[:, 2 * s:2 * s + 2], bscr)
                    inc(i, "sv", 1)
                sec(block.vector, e_ffn_ev, own="sv")
                bank_evacuated(ffnb[0], "sv")
                bank_evacuated(ffnb[1], "sv")

        def emit_out(t):
            def e_p8_ve(v, w):
                w.wait("sv")
                w.wait("ss")
                i = v.tensor_single_scalar(st4(8).unsqueeze(2), pair_v, EPS, Alu.add)
                inc(i, "sv", 1)
            sec(block.vector, e_p8_ve, own="sv")

            def e_p8_se(se, w):
                w.wait("sv")
                i = se.activation(st4(8), st4(8), Act.Sqrt)
                inc(i, "ss", 1)
            sec(block.scalar, e_p8_se, own="ss")

            def e_p8_ve2(v, w):
                w.wait("ss")
                v.reciprocal(st4(36), st4(8))
                v.drain()
                v.tensor_tensor(st4(40).unsqueeze(2), pair_m, st4(36).unsqueeze(2), Alu.mult)
                v.drain()
                i = v.tensor_single_scalar(st4(40), st4(40), -1.0, Alu.mult)
                inc(i, "sv", 1)
            sec(block.vector, e_p8_ve2, own="sv")

            for s in range(NSUB):
                def e_out_se(se, w, s=s, t=t):
                    w.wait("sv")
                    if tok_out[s] is not None:
                        wait_toks(w, [tok_out[s]])
                    i = se.activation(region(s)[:], x1sub(t, s), Act.Identity,
                                      bias=st1(40 + s), scale=st1(36 + s))
                    inc(i, "ss", 1)
                    w.wait("ss", cnt["ss"])
                    tk_ = dma(se, w, out_d[t * TQ + s * 128: t * TQ + (s + 1) * 128, :], region(s)[:], pool="st")
                    tok_out[s] = tk_
                    tok_out_all.append(tk_)
                sec(block.scalar, e_out_se, own="ss")

        emit_x_p1(0)
        emit_wsplit()
        emit_fq_dma()
        for bi in range(2):
            emit_fq_scales(bi)
            for lc in range(4):
                chv = bi * 4 + lc
                emit_fq_chunk(lc, chv)
                emit_qkv_m(0, 2 * chv)
                emit_qkv_m(0, 2 * chv + 1)
        ms["fq_done"] = cnt["sv"]
        emit_qkv_v(0)
        for t in range(NT):
            emit_scores_ln(t)
            if t < NT - 1:
                emit_x_p1(t + 1)
                for s in range(NSUB):
                    evs = [emit_qkv_m(t + 1, m, defer_ev=True)
                           for m in range(4 * s, 4 * s + 4)]
                    pre, post = (evs[0:2], evs[2:4]) if s % 2 == 0 else (evs[2:4], evs[0:2])
                    emit_quant_sub(t, s, pre_evs=pre)
                    for ev_ in post:
                        ev_()
                emit_qkv_v(t + 1)
            else:
                for s in range(NSUB):
                    emit_quant_sub(t, s)
            emit_out(t)

        def e_fin(sync, w):
            wait_toks(w, tok_out_all)
        sec(block.sync, e_fin)

    return nc


def kernel(L0_attn_x, L0_attn_W_qkv, L0_attn_b_qkv, L0_attn_ln_w, L0_attn_ln_b,
           L0_ffn_weight, L0_ffn_bias, L0_ffn_delta_w, L0_ffn_delta_act):
    x = np.ascontiguousarray(np.asarray(L0_attn_x, dtype=np.float32))
    B, T, _ = x.shape
    nc = build_nc(T)
    base = {
        "w_qkv": np.ascontiguousarray(L0_attn_W_qkv, dtype=np.float32),
        "b_qkv": np.ascontiguousarray(L0_attn_b_qkv, dtype=np.float32),
        "ln_w": np.ascontiguousarray(L0_attn_ln_w, dtype=np.float32),
        "ln_b": np.ascontiguousarray(L0_attn_ln_b, dtype=np.float32),
        "f_w": np.ascontiguousarray(L0_ffn_weight, dtype=np.float32),
        "f_b": np.ascontiguousarray(L0_ffn_bias, dtype=np.float32),
        "d_w": np.ascontiguousarray(L0_ffn_delta_w, dtype=np.float32),
        "d_a": np.ascontiguousarray(L0_ffn_delta_act, dtype=np.float32),
    }
    in_maps = [dict(base, x=x[b]) for b in range(B)]
    res = run_bass_kernel_spmd(nc, in_maps, core_ids=list(range(B)))
    out = np.stack([np.asarray(res.results[b]["out"]) for b in range(B)], axis=0)
    return out.astype(np.float32)
